# version 26
# baseline (speedup 1.0000x reference)
import math
import numpy as np
import concourse.bass as bass
import concourse.mybir as mybir
from concourse.bass_utils import run_bass_kernel_spmd

F32 = mybir.dt.float32
BF16 = mybir.dt.bfloat16
AF = mybir.ActivationFunctionType
ALU = mybir.AluOpType

NCORES = 8
B_PER_CORE = 2
S_LEN = 4096
D = 1024
H = 8
DH = 128
CHUNK = 256
NCH = S_LEN // CHUNK
NG = B_PER_CORE * NCH
TRUNC = None
DBG = None
D_FF = 4096
RANK = 256
GN_EPS = 1e-6
LN_EPS = 1e-5
ISQD = 1.0 / math.sqrt(DH)


def _consts():
    g = 1.0 - np.exp2(-5.0 - np.arange(H, dtype=np.float64))
    lg = np.log(g)
    w = np.arange(128, dtype=np.float64)
    mask = np.zeros((128, 1024), dtype=np.float64)
    for h in range(H):
        col = np.exp(lg[h] * (-(w + 1.0))) * ISQD
        tri = (w[None, :] >= w[:, None]).astype(np.float64)
        mask[:, h * 128:(h + 1) * 128] = tri * col[:, None]
    qcol = np.zeros((128, 2048), dtype=np.float64)
    for h in range(H):
        pat = np.exp(lg[h] * ((w % 128) + 1.0))
        qcol[:, h * 256:h * 256 + 128] = pat[None, :]
        qcol[:, h * 256 + 128:(h + 1) * 256] = pat[None, :]
    kdcol = np.zeros((128, 1024), dtype=np.float64)
    for h in range(H):
        colk = np.exp(lg[h] * (127.0 - w)) * ISQD
        kdcol[:, h * 128:(h + 1) * 128] = colk[:, None]
    gid = np.zeros((128, 1024), dtype=np.float64)
    for h in range(H):
        gid[:, h * 128:(h + 1) * 128] = np.exp(lg[h] * 128.0) * np.eye(128)
    ident = np.eye(128, dtype=np.float64)
    return (mask.astype(np.float32), qcol.astype(np.float32),
            kdcol.astype(np.float32), gid.astype(np.float32),
            ident.astype(np.float32))


class _Plan:
    def __init__(self):
        self.ops = {"tensor": [], "vector": [], "scalar": [], "sync": [],
                    "gpsimd": []}

    def emit(self, eng, fn):
        self.ops[eng].append(fn)


def build_program():
    nc = bass.Bass("TRN2", target_bir_lowering=False, detect_race_conditions=False)
    X = nc.dram_tensor("X", [B_PER_CORE, S_LEN, D], F32, kind="ExternalInput").ap()
    XT = nc.dram_tensor("XT", [B_PER_CORE, D, S_LEN], F32, kind="ExternalInput").ap()
    OUT = nc.dram_tensor("OUT", [B_PER_CORE, S_LEN, D], F32,
                         kind="ExternalOutput").ap()
    Wq = nc.dram_tensor("Wq", [D, D], F32, kind="ExternalInput").ap()
    Wk = nc.dram_tensor("Wk", [D, D], F32, kind="ExternalInput").ap()
    Wv = nc.dram_tensor("Wv", [D, D], F32, kind="ExternalInput").ap()
    Wg = nc.dram_tensor("Wg", [D, D], F32, kind="ExternalInput").ap()
    Wo = nc.dram_tensor("Wo", [D, D], F32, kind="ExternalInput").ap()
    U1 = nc.dram_tensor("U1", [D, RANK], F32, kind="ExternalInput").ap()
    V1 = nc.dram_tensor("V1", [RANK, D_FF], F32, kind="ExternalInput").ap()
    B1 = nc.dram_tensor("B1", [D_FF], F32, kind="ExternalInput").ap()
    U2 = nc.dram_tensor("U2", [D_FF, RANK], F32, kind="ExternalInput").ap()
    V2 = nc.dram_tensor("V2", [RANK, D], F32, kind="ExternalInput").ap()
    B2 = nc.dram_tensor("B2", [D], F32, kind="ExternalInput").ap()

    maskC, qcolC, kdcolC, gidC, idC = _consts()
    d_mask = nc.inline_tensor(maskC, "d_mask").ap()
    d_qcol = nc.inline_tensor(qcolC, "d_qcol").ap()
    d_kdcol = nc.inline_tensor(kdcolC, "d_kdcol").ap()
    d_gid = nc.inline_tensor(gidC, "d_gid").ap()
    d_id = nc.inline_tensor(idC, "d_id").ap()

    from contextlib import ExitStack
    es = ExitStack()

    def sb(name, shape, dtype):
        return es.enter_context(nc.sbuf_tensor(name, shape, dtype))

    w_q = sb("w_q", [128, 8, 1024], BF16)
    w_k = sb("w_k", [128, 8, 1024], BF16)
    w_v = sb("w_v", [128, 8, 1024], BF16)
    w_g = sb("w_g", [128, 8, 1024], BF16)
    w_o = sb("w_o", [128, 8, 1024], BF16)
    w_u1 = sb("w_u1", [128, 8, 256], BF16)
    w_v1 = sb("w_v1", [128, 2, 4096], BF16)
    w_u2 = sb("w_u2", [128, 32, 256], BF16)
    w_v2 = sb("w_v2", [128, 2, 1024], BF16)
    r_b2 = sb("r_b2", [1, 1024], BF16)
    r_ones = sb("r_ones", [1, 128], BF16)
    c_onec = sb("c_onec", [128, 1], BF16)
    c_u1s = sb("c_u1s", [128, 2], F32)
    r_u1s = sb("r_u1s", [1, 256], BF16)
    b_rstd = sb("b_rstd", [128, 256], F32)
    c_mask = sb("c_mask", [128, 1024], BF16)
    c_qcol = sb("c_qcol", [128, 2048], BF16)
    c_kdcol = sb("c_kdcol", [128, 1024], BF16)
    c_gid = sb("c_gid", [128, 1024], F32)
    c_idf = sb("c_idf", [128, 128], F32)
    c_b1c = sb("c_b1c", [128, 32], F32)
    c_idb = sb("c_idb", [128, 128], BF16)
    a_x0 = sb("a_x0", [128, 2, 2, 1024], F32)
    a_x0t2 = sb("a_x0t2", [128, 2, 8, 256], BF16)
    a_qt = sb("a_qt", [128, 8, 256], BF16)
    a_kt = sb("a_kt", [128, 8, 256], BF16)
    a_x1t = sb("a_x1t", [128, 8, 256], BF16)
    a_v = sb("a_v", [128, 2, 1024], BF16)
    a_kd = sb("a_kd", [128, 1024], BF16)
    a_am = sb("a_am", [128, 1024], BF16)
    a_sf = sb("a_sf", [128, 1024], F32)
    a_sb = sb("a_sb", [128, 1024], BF16)
    a_y = sb("a_y", [128, 2, 1024], BF16)
    a_u = sb("a_u", [128, 2, 1024], BF16)
    a_gt = sb("a_gt", [128, 8, 256], BF16)
    a_y0 = sb("a_y0", [128, 2, 1024], F32)
    a_gated = a_u
    a_ht = a_u
    a_p = a_y[:, :, 0:256]
    a_qb = a_y[:, :, 256:512]
    s_gn = sb("s_gn", [128, 48], F32)
    s_gn2 = sb("s_gn2", [128, 40], F32)
    s_ln = sb("s_ln", [128, 24], F32)
    s_ln2 = sb("s_ln2", [128, 16], F32)
    s_lnB = sb("s_lnB", [128, 24], F32)
    s_ln2B = sb("s_ln2B", [128, 16], F32)

    pA = es.enter_context(nc.psum_tensor("pA", [128, 1024], F32))
    pC = es.enter_context(nc.psum_tensor("pC", [128, 1024], F32))
    pS = es.enter_context(nc.psum_tensor("pS", [128, 1024], F32))
    pT = es.enter_context(nc.psum_tensor("pT", [128, 2048], BF16))

    sem = {}
    for name in ("LD0", "LD1", "GT0", "GT1", "WT", "CN", "ST", "T", "V", "S"):
        sem[name] = es.enter_context(nc.semaphore(name))

    plan = _Plan()
    cnt = {"LD0": 0, "LD1": 0, "GT0": 0, "GT1": 0, "WT": 0, "CN": 0,
           "ST": 0, "T": 0, "V": 0, "S": 0}

    def inc(eng_name, sem_name, fn):
        cnt[sem_name] += 1
        s = sem[sem_name]
        plan.emit(eng_name, lambda e, fn=fn, s=s: fn(e).then_inc(s, 1))
        return cnt[sem_name]

    wait_done = {}

    def wait(eng_name, sem_name, val):
        if val <= 0:
            return
        key = (eng_name, sem_name)
        if wait_done.get(key, -1) >= val:
            return
        wait_done[key] = val
        s = sem[sem_name]
        plan.emit(eng_name, lambda e, s=s, val=val: e.wait_ge(s, val))

    t_do = lambda fn: plan.emit("tensor", fn)
    v_do = lambda fn: plan.emit("vector", fn)
    s_do = lambda fn: plan.emit("scalar", fn)
    v_inc = lambda fn: inc("vector", "V", fn)
    s_inc = lambda fn: inc("scalar", "S", fn)

    def mark(eng_name, sem_name):
        cnt[sem_name] += 1
        s = sem[sem_name]
        last = plan.ops[eng_name].pop()
        plan.emit(eng_name,
                  lambda e, last=last, s=s: last(e).then_inc(s, 1))
        return cnt[sem_name]

    t_inc = lambda: mark("tensor", "T")
    s_mark = lambda: mark("scalar", "S")

    pair_free = {"pA": None, "pC": None, "pS": None,
                 "pT0": None, "pT1": None}

    def t_need(pname):
        wv = pair_free[pname]
        if wv is not None:
            wait("tensor", wv[0], wv[1])

    def release(pname, sem_name, val):
        pair_free[pname] = (sem_name, val)

    def gload(dst, src):
        cnt["WT"] += 16
        s = sem["WT"]
        plan.emit("gpsimd", lambda e, dst=dst, src=src, s=s: e.dma_start(
            out=dst, in_=src).then_inc(s, 16))

    gload(w_q[:, :, :], Wq.rearrange("(k p) j -> p k j", p=128))
    gload(w_k[:, :, :], Wk.rearrange("(k p) j -> p k j", p=128))
    gload(w_v[:, :, :], Wv.rearrange("(k p) j -> p k j", p=128))
    gload(w_g[:, :, :], Wg.rearrange("(k p) j -> p k j", p=128))
    gload(w_o[:, :, :], Wo.rearrange("(k p) j -> p k j", p=128))
    gload(w_u1[:, :, :], U1.rearrange("(k p) j -> p k j", p=128))
    gload(w_v1[:, :, :], V1.rearrange("(k p) j -> p k j", p=128))
    gload(w_u2[:, :, :], U2.rearrange("(k p) j -> p k j", p=128))
    gload(w_v2[:, :, :], V2.rearrange("(k p) j -> p k j", p=128))
    gload(r_b2[:, :], B2.rearrange("(a j) -> a j", a=1))
    gload(c_mask[:, :], d_mask)
    gload(c_qcol[:, :], d_qcol)
    gload(c_kdcol[:, :], d_kdcol)
    WT_ALL = cnt["WT"]

    def sload(dst, src):
        cnt["CN"] += 16
        s = sem["CN"]
        plan.emit("sync", lambda e, dst=dst, src=src, s=s: e.dma_start(
            out=dst, in_=src).then_inc(s, 16))
        return cnt["CN"]

    sload(c_gid[:, :], d_gid)
    for _t in range(32):
        sload(c_b1c[:, _t:_t + 1],
              B1[_t * 128:(_t + 1) * 128].rearrange("(p a) -> p a", a=1))
    LD_CONST = sload(c_idf[:, :], d_id)
    CN_ALL = cnt["CN"]

    v_onec = v_inc(lambda e: e.memset(c_onec[:, :], 1.0))
    v_ones = v_inc(lambda e: e.memset(r_ones[:, :], 1.0))
    wait("scalar", "CN", LD_CONST)
    s_idb = s_inc(lambda e: e.activation(out=c_idb[:, :], in_=c_idf[:, :],
                                         func=AF.Copy))

    t_need("pA")
    wait("tensor", "WT", WT_ALL)
    wait("tensor", "CN", LD_CONST)
    wait("tensor", "V", v_onec)
    for m in range(2):
        for k in range(8):
            t_do(lambda e, m=m, k=k: e.matmul(
                pA[:, m:m + 1], w_u1[:, k, m * 128:(m + 1) * 128],
                c_onec[:, :], start=(k == 0), stop=(k == 7)))
    tu1 = t_inc()
    wait("vector", "T", tu1)
    vus = v_inc(lambda e: e.tensor_copy(out=c_u1s[:, :], in_=pA[:, 0:2]))
    wait("tensor", "V", vus)
    for m in range(2):
        t_do(lambda e, m=m: e.transpose(
            pA[0:1, 128 + m * 256:256 + m * 256], c_u1s[:, m:m + 1],
            c_idf[:, :]))
    tu2 = t_inc()
    wait("scalar", "T", tu2)
    for m in range(2):
        s_do(lambda e, m=m: e.activation(
            out=r_u1s[0:1, m * 128:(m + 1) * 128],
            in_=pA[0:1, 128 + m * 256:256 + m * 256], func=AF.Copy))
    su_ = s_mark()
    release("pA", "S", su_)

    def x0_load(g):
        b, c = divmod(g, NCH)
        t0 = c * CHUNK
        slot = g % 2
        sname = "LD0" if slot == 0 else "LD1"
        src = X[b, t0:t0 + CHUNK, :].rearrange("(a p) j -> p a j", p=128)
        cnt[sname] += 16
        s = sem[sname]
        plan.emit("sync", lambda e, src=src, slot=slot, s=s: e.dma_start(
            out=a_x0[:, slot, :, :], in_=src).then_inc(s, 16))
        return (sname, cnt[sname])

    proj_done = {}

    def ln_combine(SL=None, SL2=None):
        wait("vector", "V", cnt["V"])
        SL = s_ln if SL is None else SL
        SL2 = s_ln2 if SL2 is None else SL2
        L = SL[:, :]
        me0 = L[:, 1:14:12]
        mo0 = L[:, 4:17:12]
        me1 = L[:, 7:20:12]
        mo1 = L[:, 10:23:12]
        Me0 = L[:, 2:15:12]
        Mo0 = L[:, 5:18:12]
        Me1 = L[:, 8:21:12]
        Mo1 = L[:, 11:24:12]
        mu2 = SL2[:, 0:2]
        var2 = SL2[:, 2:4]
        rst2 = SL2[:, 4:6]
        ngm2 = SL2[:, 6:8]
        m1_ = SL2[:, 8:10]
        m2_ = SL2[:, 10:12]
        d1_ = SL2[:, 12:14]
        d2_ = SL2[:, 14:16]
        v_do(lambda e: e.tensor_add(m1_, me0, mo0))
        v_do(lambda e: e.tensor_add(m2_, me1, mo1))
        v_do(lambda e: e.tensor_sub(d1_, me0, mo0))
        v_do(lambda e: e.tensor_sub(d2_, me1, mo1))
        v_do(lambda e: e.tensor_add(var2, Me0, Mo0))
        v_do(lambda e: e.tensor_add(mu2, Me1, Mo1))
        v_do(lambda e: e.drain())
        v_do(lambda e: e.scalar_tensor_tensor(
            out=d1_, in0=d1_, scalar=0.125, in1=d1_, op0=ALU.mult,
            op1=ALU.mult))
        v_do(lambda e: e.scalar_tensor_tensor(
            out=d2_, in0=d2_, scalar=0.125, in1=d2_, op0=ALU.mult,
            op1=ALU.mult))
        v_do(lambda e: e.tensor_add(var2, var2, mu2))
        v_do(lambda e: e.drain())
        v_do(lambda e: e.tensor_sub(m2_, m1_, m2_))
        v_do(lambda e: e.tensor_scalar(
            out=var2, in0=var2, scalar1=1.0 / 1024.0, scalar2=LN_EPS,
            op0=ALU.mult, op1=ALU.add))
        v_do(lambda e: e.drain())
        v_do(lambda e: e.scalar_tensor_tensor(
            out=mu2, in0=m1_, scalar=2.0, in1=m2_, op0=ALU.mult,
            op1=ALU.subtract))
        v_do(lambda e: e.drain())
        v_do(lambda e: e.scalar_tensor_tensor(
            out=m2_, in0=m2_, scalar=0.0625, in1=m2_, op0=ALU.mult,
            op1=ALU.mult))
        v_do(lambda e: e.tensor_add(var2, var2, d1_))
        v_do(lambda e: e.drain())
        v_do(lambda e: e.tensor_scalar_mul(mu2, mu2, -0.25))
        v_do(lambda e: e.tensor_add(var2, var2, d2_))
        v_do(lambda e: e.drain())
        v_do(lambda e: e.tensor_add(var2, var2, m2_))
        v_do(lambda e: e.drain())
        vr = v_inc(lambda e: e.reciprocal(out=rst2, in_=var2))
        wait("scalar", "V", vr)
        sq = s_inc(lambda e: e.activation(out=rst2, in_=rst2,
                                          func=AF.Sqrt))
        wait("vector", "S", sq)
        return v_inc(lambda e: e.tensor_mul(ngm2, mu2, rst2))

    qk_done = {}
    pending_tail = [None]

    def x0t_load(g):
        b, c = divmod(g, NCH)
        t0 = c * CHUNK
        slot = g % 2
        sname = "GT0" if slot == 0 else "GT1"
        srcap = XT[b, :, t0:t0 + CHUNK].rearrange("(k p) t -> p k t", p=128)
        cnt[sname] += 16
        s = sem[sname]
        wt_t = proj_done.get(g - 2, 0)
        st_ = sem["T"]
        plan.emit("gpsimd", lambda e, srcap=srcap, slot=slot, s=s, wt_t=wt_t,
                  st_=st_: (
                      e.wait_ge(st_, wt_t) if wt_t else None,
                      e.dma_start(out=a_x0t2[:, slot, :, :], in_=srcap)
                      .then_inc(s, 16))[-1])
        return (sname, cnt[sname])

    ldt_of = {0: x0t_load(0), 1: x0t_load(1)}
    ld_of = {0: x0_load(0), 1: x0_load(1)}
    st_of = {}
    v_state_ready = 0
    sgn_prev = [0]
    v_out_apply = {}

    dbg_state = {"fired": False}

    def dbg_dump(stage, bufs, wsem, wval):
        if DBG != stage or dbg_state["fired"]:
            return False
        dbg_state["fired"] = True
        row = 0
        for ap_ in bufs:
            cnt["WT"] += 16
            s = sem["WT"]
            n = 1
            sh = ap_.shape
            tot = 1
            for d_ in sh[1:]:
                tot *= d_
            nrow = (tot + 1023) // 1024
            lastd = sh[-1] if len(sh) <= 2 else 1024
            dst = (OUT[0, row:row + 128 * nrow, :].rearrange(
                "(a p) j -> p a j", p=128) if nrow > 1
                else OUT[0, row:row + 128, 0:tot])
            wv, ws2, wt2 = cnt["V"], cnt["S"], cnt["T"]
            plan.emit("gpsimd", lambda e, dst=dst, ap_=ap_, s=s, wv=wv,
                      ws2=ws2, wt2=wt2: (
                          e.wait_ge(sem["V"], wv) if wv else None,
                          e.wait_ge(sem["S"], ws2) if ws2 else None,
                          e.wait_ge(sem["T"], wt2) if wt2 else None,
                          e.dma_start(out=dst, in_=ap_).then_inc(s, 16))[-1])
            row += 128 * nrow
        return True

    for g in range(NG):
        b, c = divmod(g, NCH)
        slot = g % 2
        x0 = a_x0[:, slot, :, :]

        if c == 0:
            wait("vector", "T", cnt["T"])
            v_inc(lambda e: e.memset(a_sf[:, :], 0.0))
            v_state_ready = v_inc(lambda e: e.memset(a_sb[:, :], 0.0))

        wait("tensor", ld_of[g][0], ld_of[g][1])
        if g == 0:
            wait("tensor", "WT", WT_ALL)
            wait("tensor", "CN", CN_ALL)
            wait("tensor", "S", s_idb)
            wait("tensor", "V", v_ones)

        def xt_of(gq):
            return a_x0t2[:, gq % 2, :, :]

        def qk_mms(gq, wsb, half, pair, pname):
            wait("tensor", ldt_of[gq][0], ldt_of[gq][1])
            t_need(pname)
            xt = xt_of(gq)
            for hh in range(half * 4, half * 4 + 4):
                for k in range(8):
                    t_do(lambda e, hh=hh, k=k, xt=xt, wsb=wsb, pair=pair:
                         e.matmul(
                             pair[:, (hh % 4) * 256:(hh % 4) * 256 + 256],
                             wsb[:, k, hh * 128:(hh + 1) * 128],
                             xt[:, k, :], start=(k == 0), stop=(k == 7)))
            return t_inc()

        def q_evac(half, pair, pname, tq):
            wait("vector", "T", tq)
            vq = v_inc(lambda e, half=half, pair=pair: e.tensor_tensor(
                out=a_qt[:, half * 4:half * 4 + 4, :],
                in0=pair[:, :].rearrange("p (h j) -> p h j", j=256),
                in1=c_qcol[:, half * 1024:(half + 1) * 1024]
                .rearrange("p (h j) -> p h j", j=256),
                op=ALU.mult))
            release(pname, "V", vq)
            return vq

        def k_evac(half, pair, pname, tk):
            wait("scalar", "T", tk)
            sk = s_inc(lambda e, half=half, pair=pair: e.activation(
                out=a_kt[:, half * 4:half * 4 + 4, :]
                .rearrange("p h j -> p (h j)"),
                in_=pair[:, :], func=AF.Copy))
            release(pname, "S", sk)
            return sk

        def emit_qk(gq):
            tq0 = qk_mms(gq, w_q, 0, pA, "pA")
            q_evac(0, pA, "pA", tq0)
            tq1 = qk_mms(gq, w_q, 1, pA, "pA")
            vq = q_evac(1, pA, "pA", tq1)
            tk0 = qk_mms(gq, w_k, 0, pA, "pA")
            k_evac(0, pA, "pA", tk0)
            tk1 = qk_mms(gq, w_k, 1, pA, "pA")
            sk = k_evac(1, pA, "pA", tk1)
            qk_done[gq] = (vq, sk)

        def tokproj_T(wsb, pair, t, gq):
            xt = xt_of(gq)
            for nk in range(2):
                for k in range(8):
                    t_do(lambda e, nk=nk, k=k, wsb=wsb, pair=pair, t=t,
                         xt=xt:
                         e.matmul(pair[:, nk * 512:(nk + 1) * 512],
                                  xt[:, k, t * 128:(t + 1) * 128],
                                  wsb[:, k, nk * 512:(nk + 1) * 512],
                                  start=(k == 0), stop=(k == 7)))
            return t_inc()

        if g == 0:
            emit_qk(0)
        vqB, skB = qk_done[g]

        def ktr(r, pTn):
            pth = pT[:, r * 1024:(r + 1) * 1024]
            t_need(pTn)
            wait("tensor", "S", skB)
            for h in range(8):
                t_do(lambda e, h=h, r=r, pth=pth: e.transpose(
                    pth[:, h * 128:(h + 1) * 128],
                    a_kt[:, h, r * 128:(r + 1) * 128], c_idb[:, :]))
            tkt = t_inc()
            wait("vector", "T", tkt)
            vkd = v_inc(lambda e, pth=pth: e.tensor_tensor(
                out=a_kd[:, :], in0=pth[:, :], in1=c_kdcol[:, :],
                op=ALU.mult))
            release(pTn, "V", vkd)
            return vkd

        def a_mms(r, pair, pname):
            t_need(pname)
            wait("tensor", "V", vqB)
            for h in range(8):
                t_do(lambda e, h=h, r=r, pair=pair: e.matmul(
                    pair[:, h * 128:(h + 1) * 128],
                    a_kt[:, h, r * 128:(r + 1) * 128],
                    a_qt[:, h, r * 128:(r + 1) * 128],
                    start=True, stop=True))
            ta = t_inc()
            wait("vector", "T", ta)
            vam = v_inc(lambda e, pair=pair: e.tensor_tensor(
                out=a_am[:, :], in0=pair[:, :], in1=c_mask[:, :],
                op=ALU.mult))
            release(pname, "V", vam)
            return vam

        def ret_mms(r, pair, pname, vam, sv_r):
            t_need(pname)
            wait("tensor", "V", vam)
            wait("tensor", "V", v_state_ready)
            wait("tensor", "S", sv_r)
            for h in range(8):
                t_do(lambda e, h=h, r=r, pair=pair: e.matmul(
                    pair[:, h * 128:(h + 1) * 128],
                    a_am[:, h * 128:(h + 1) * 128],
                    a_v[:, r, h * 128:(h + 1) * 128],
                    start=True, stop=False))
                t_do(lambda e, h=h, r=r, pair=pair: e.matmul(
                    pair[:, h * 128:(h + 1) * 128],
                    a_qt[:, h, r * 128:(r + 1) * 128],
                    a_sb[:, h * 128:(h + 1) * 128],
                    start=False, stop=True))
            return t_inc()

        def state_mms(r, vkd):
            t_need("pS")
            wait("tensor", "V", vkd)
            for h in range(8):
                t_do(lambda e, h=h: e.matmul(
                    pS[:, h * 128:(h + 1) * 128],
                    c_gid[:, h * 128:(h + 1) * 128],
                    a_sf[:, h * 128:(h + 1) * 128],
                    start=True, stop=False))
                t_do(lambda e, h=h, r=r: e.matmul(
                    pS[:, h * 128:(h + 1) * 128],
                    a_kd[:, h * 128:(h + 1) * 128],
                    a_v[:, r, h * 128:(h + 1) * 128],
                    start=False, stop=True))
            tst = t_inc()
            wait("vector", "T", tst)
            v_inc(lambda e: e.tensor_copy(out=a_sf[:, :], in_=pS[:, :]))
            vsr = v_inc(lambda e: e.tensor_copy(out=a_sb[:, :], in_=pS[:, :]))
            release("pS", "V", vsr)
            return tst, vsr

        def gn_block(r, pair, pname, tret, mid_v=None, mid_apply=None):
            wait("vector", "T", tret)
            for h in range(7):
                v_do(lambda e, h=h, pair=pair: e.bn_stats(
                    out=s_gn[:, h * 6:(h + 1) * 6],
                    in_=pair[:, h * 128:(h + 1) * 128]))
            vbn = v_inc(lambda e, pair=pair: e.bn_stats(
                out=s_gn[:, 42:48], in_=pair[:, 896:1024]))
            wait("vector", "V", vbn)
            gn = s_gn[:, :].rearrange("p (h s) -> p h s", s=6)
            mu_ = s_gn2[:, 0:8]
            var_ = s_gn2[:, 8:16]
            rst_ = s_gn2[:, 16:24]
            ngm_ = s_gn2[:, 24:32]
            tmp_ = s_gn2[:, 32:40]
            v_do(lambda e, gn=gn: e.tensor_sub(tmp_, gn[:, :, 1], gn[:, :, 4]))
            v_do(lambda e, gn=gn: e.tensor_add(var_, gn[:, :, 2], gn[:, :, 5]))
            v_do(lambda e, gn=gn: e.tensor_add(mu_, gn[:, :, 1], gn[:, :, 4]))
            v_do(lambda e: e.drain())
            v_do(lambda e: e.scalar_tensor_tensor(
                out=tmp_, in0=tmp_, scalar=0.25, in1=tmp_,
                op0=ALU.mult, op1=ALU.mult))
            v_do(lambda e: e.tensor_scalar_mul(mu_, mu_, 0.5))
            v_do(lambda e: e.tensor_scalar(
                out=var_, in0=var_, scalar1=1.0 / 128.0, scalar2=GN_EPS,
                op0=ALU.mult, op1=ALU.add))
            v_do(lambda e: e.drain())
            v_do(lambda e: e.tensor_add(var_, var_, tmp_))
            v_do(lambda e: e.drain())
            vrec = v_inc(lambda e: e.reciprocal(out=rst_, in_=var_))
            wait("scalar", "V", vrec)
            srt = s_inc(lambda e: e.activation(out=rst_, in_=rst_,
                                               func=AF.Sqrt))
            if mid_v is not None:
                mid_v()
            wait("vector", "S", srt)
            v_do(lambda e: e.scalar_tensor_tensor(
                out=ngm_, in0=mu_, scalar=-1.0, in1=rst_,
                op0=ALU.mult, op1=ALU.mult))
            v_do(lambda e: e.drain())
            for h in range(8):
                v_do(lambda e, h=h, r=r, pair=pair: e.tensor_scalar(
                    out=a_y[:, r, h * 128:(h + 1) * 128],
                    in0=pair[:, h * 128:(h + 1) * 128],
                    scalar1=s_gn2[:, 16 + h:17 + h],
                    scalar2=s_gn2[:, 24 + h:25 + h],
                    op0=ALU.mult, op1=ALU.add))
                if h == 3 and mid_apply is not None:
                    mid_apply()
            vap = mark("vector", "V")
            release(pname, "V", vap)
            return vap

        def g_proj(t, pname, gq):
            t_need(pname)
            tgp = tokproj_T(w_g, pA, t, gq)
            wait("scalar", "T", tgp)
            sg = s_inc(lambda e, t=t: e.activation(
                out=a_u[:, t, :], in_=pA[:, :], func=AF.Tanh, scale=0.5))
            wait("vector", "S", sg)
            vu = v_inc(lambda e, t=t: e.scalar_tensor_tensor(
                out=a_u[:, t, :], in0=a_u[:, t, :], scalar=1.0,
                in1=pA[:, :], op0=ALU.add, op1=ALU.mult))
            release(pname, "V", vu)
            return vu

        def gated_tr(t, vg):
            pth = pT[:, t * 1024:(t + 1) * 1024]
            t_need("pT%d" % t)
            wait("tensor", "V", vg)
            for k in range(8):
                t_do(lambda e, k=k, t=t, pth=pth: e.transpose(
                    pth[:, k * 128:(k + 1) * 128],
                    a_u[:, t, k * 128:(k + 1) * 128], c_idb[:, :]))
            tgt = t_inc()
            wait("scalar", "T", tgt)
            s_ = s_inc(lambda e, t=t, pth=pth: e.activation(
                out=a_gt[:, :, t * 128:(t + 1) * 128],
                in_=pth[:, :].rearrange("p (k j) -> p k j", j=128),
                func=AF.Copy))
            release("pT%d" % t, "S", s_)
            return s_

        ln = s_ln[:, :].rearrange("p (t u s) -> p t u s", u=2, s=6)
        lnB = s_lnB[:, :].rearrange("p (t u s) -> p t u s", u=2, s=6)

        def attnout(t, pair, pname, sgt_t, vhalf=None):
            if vhalf is not None:
                wv = pair_free[pname]
                wait("tensor", "S", sgt_t)
                wait("tensor", "V", vhalf)
                for k in range(8):
                    t_do(lambda e, k=k, pair=pair, t=t: e.matmul(
                        pair[:, 0:512],
                        a_gt[:, k, t * 128:(t + 1) * 128],
                        w_o[:, k, 0:512],
                        start=(k == 0), stop=(k == 7)))
                if wv is not None:
                    wait("tensor", wv[0], wv[1])
                for k in range(8):
                    t_do(lambda e, k=k, pair=pair, t=t: e.matmul(
                        pair[:, 512:1024],
                        a_gt[:, k, t * 128:(t + 1) * 128],
                        w_o[:, k, 512:1024],
                        start=(k == 0), stop=(k == 7)))
            else:
                t_need(pname)
                wait("tensor", "S", sgt_t)
                for nk in range(2):
                    for k in range(8):
                        t_do(lambda e, nk=nk, k=k, pair=pair, t=t: e.matmul(
                            pair[:, nk * 512:(nk + 1) * 512],
                            a_gt[:, k, t * 128:(t + 1) * 128],
                            w_o[:, k, nk * 512:(nk + 1) * 512],
                            start=(k == 0), stop=(k == 7)))
            tao = t_inc()
            wait("vector", "T", tao)
            vy = v_inc(lambda e, pair=pair, t=t, x0=x0: e.scalar_tensor_tensor(
                out=a_y0[:, t, :], in0=pair[:, :], scalar=1.0,
                in1=x0[:, t, :], op0=ALU.mult, op1=ALU.add))
            release(pname, "V", vy)
            v_do(lambda e: e.drain())
            for u_ in range(2):
                v_inc(lambda e, t=t, u_=u_, ln=ln: e.bn_stats(
                    out=ln[:, t, u_, :],
                    in_=a_y0[:, t, u_ * 512:(u_ + 1) * 512]))
            return vy

        vkd0 = ktr(0, "pT0")
        t_need("pA")
        tvp0 = tokproj_T(w_v, pA, 0, g)
        wait("scalar", "T", tvp0)
        sv0 = s_inc(lambda e: e.activation(out=a_v[:, 0, :], in_=pA[:, :],
                                           func=AF.Copy))
        release("pA", "S", sv0)
        vam0 = a_mms(0, pC, "pC")
        t_need("pA")
        tvp1 = tokproj_T(w_v, pA, 1, g)
        wait("scalar", "T", tvp1)
        sv1 = s_inc(lambda e: e.activation(out=a_v[:, 1, :], in_=pA[:, :],
                                           func=AF.Copy))
        release("pA", "S", sv1)
        tret0 = ret_mms(0, pC, "pC", vam0, sv0)
        tst0, v_state_ready = state_mms(0, vkd0)
        vkd1 = ktr(1, "pT1")
        gn_block(0, pC, "pC", tret0)
        vu0 = g_proj(0, "pA", g)
        v_do(lambda e: e.drain())
        vg0 = v_inc(lambda e: e.scalar_tensor_tensor(
            out=a_u[:, 0, :], in0=a_u[:, 0, :], scalar=0.5,
            in1=a_y[:, 0, :], op0=ALU.mult, op1=ALU.mult))
        vam1 = a_mms(1, pS, "pS")
        vu1 = g_proj(1, "pA", g)
        proj_done[g] = cnt["T"]
        if g + 2 < NG:
            ldt_of[g + 2] = x0t_load(g + 2)
        tret1 = ret_mms(1, pC, "pC", vam1, sv1)
        gatedT0 = gated_tr(0, vg0)
        tst1, v_state_ready = state_mms(1, vkd1)
        nxt = g + 1 < NG
        if nxt:
            tqa = qk_mms(g + 1, w_q, 0, pA, "pA")
            mid = lambda: q_evac(0, pA, "pA", tqa)
        else:
            mid = None
        vapA_box = [None]

        def mid_apply():
            vapA_box[0] = mark("vector", "V")

        gn_block(1, pC, "pC", tret1, mid_v=mid, mid_apply=mid_apply)
        v_do(lambda e: e.drain())
        vg1 = v_inc(lambda e: e.scalar_tensor_tensor(
            out=a_u[:, 1, :], in0=a_u[:, 1, :], scalar=0.5,
            in1=a_y[:, 1, :], op0=ALU.mult, op1=ALU.mult))
        if nxt:
            tqb = qk_mms(g + 1, w_q, 1, pS, "pS")
        gatedT1 = gated_tr(1, vg1)
        if nxt:
            tka = qk_mms(g + 1, w_k, 0, pA, "pA")
            vqn = q_evac(1, pS, "pS", tqb)
            skn0 = k_evac(0, pA, "pA", tka)
        if pending_tail[0] is not None:
            pending_tail[0]()
            pending_tail[0] = None
        vy00 = attnout(0, pC, "pC", gatedT0, vhalf=vapA_box[0])
        vy01 = attnout(1, pS, "pS", gatedT1)
        if nxt:
            tkb = qk_mms(g + 1, w_k, 1, pA, "pA")
            skn = k_evac(1, pA, "pA", tkb)
            qk_done[g + 1] = (vqn, skn)

        vln1 = ln_combine()

        vx1t = []
        for t, pair, pname in ((0, pC, "pC"), (1, pS, "pS")):
            t_need(pname)
            for k in range(8):
                t_do(lambda e, k=k, t=t, pair=pair: e.transpose(
                    pair[:, k * 128:(k + 1) * 128],
                    a_y0[:, t, k * 128:(k + 1) * 128], c_idf[:, :]))
            tx1 = t_inc()
            wait("scalar", "T", tx1)
            s_ = s_inc(lambda e, pair=pair, t=t: e.activation(
                out=a_x1t[:, :, t * 128:(t + 1) * 128],
                in_=pair[:, :].rearrange("p (k j) -> p k j", j=128),
                func=AF.Copy))
            vx1t.append(s_)
            release(pname, "S", s_)

        t_need("pS")
        wait("tensor", "S", vx1t[1])
        for k in range(8):
            t_do(lambda e, k=k: e.matmul(
                pS[:, 0:256], w_u1[:, k, 0:128],
                a_x1t[:, k, :], start=(k == 0), stop=False))
        t_need("pA")
        wait("tensor", "V", vln1)
        for i_, sc in enumerate((0, 1, 4, 5)):
            t_do(lambda e, i_=i_, sc=sc: e.transpose(
                pA[0:1, 256 + (i_ % 2) * 128 + (i_ // 2) * 512:
                   384 + (i_ % 2) * 128 + (i_ // 2) * 512],
                s_ln2[:, sc:sc + 1], c_idf[:, :]))
        trr = t_inc()
        wait("vector", "T", trr)
        v_do(lambda e: e.tensor_copy(out=a_am[0:1, 0:256],
                                     in_=pA[0:1, 256:512]))
        v_do(lambda e: e.tensor_copy(out=a_am[0:1, 256:512],
                                     in_=pA[0:1, 768:1024]))
        vrr = mark("vector", "V")
        wait("tensor", "V", vrr)
        t_do(lambda e: e.matmul(
            pS[:, 0:256], r_u1s[0:1, 0:128],
            a_am[0:1, 0:256], start=False, stop=True))
        for k in range(8):
            t_do(lambda e, k=k: e.matmul(
                pS[:, 256:512], w_u1[:, k, 128:256],
                a_x1t[:, k, :], start=(k == 0), stop=False))
        t_do(lambda e: e.matmul(
            pA[:, 256:512], r_ones[0:1, 0:128],
            a_am[0:1, 256:512], start=True, stop=True))
        tb_ = t_inc()
        t_do(lambda e: e.matmul(
            pS[:, 256:512], r_u1s[0:1, 128:256],
            a_am[0:1, 0:256], start=False, stop=True))
        tp = t_inc()
        wait("scalar", "T", tb_)
        se_ = s_inc(lambda e: e.activation(out=b_rstd[:, :],
                                           in_=pA[:, 256:512], func=AF.Copy))
        wait("vector", "S", se_)
        wait("vector", "T", tp)
        v_do(lambda e: e.tensor_mul(a_p[:, 0, :], pS[:, 0:256],
                                    b_rstd[:, :]))
        vp = v_inc(lambda e: e.tensor_mul(a_p[:, 1, :], pS[:, 256:512],
                                          b_rstd[:, :]))
        release("pS", "V", vp)
        for t in range(2):
            v_inc(lambda e, t=t: e.tensor_scalar(
                out=a_y0[:, t, :], in0=a_y0[:, t, :],
                scalar1=s_ln2[:, 4 + t:5 + t],
                scalar2=s_ln2[:, 6 + t:7 + t],
                op0=ALU.mult, op1=ALU.add))
        slots = [(pC, "pC", 0), (pC, "pC", 1), (pS, "pS", 0), (pS, "pS", 1)]
        slot_free = {0: None, 1: None, 2: None, 3: None}

        def v1_blk(blk):
            pair, pname, hi = slots[blk % 4]
            w = slot_free[blk % 4]
            if w is not None:
                wait("tensor", w[0], w[1])
            elif pname == "pC":
                wv = pair_free["pC"]
                if wv is not None:
                    wait("tensor", wv[0], wv[1])
            else:
                wv = pair_free["pS"]
                if wv is not None:
                    wait("tensor", wv[0], wv[1])
            wait("tensor", "V", vp)
            for ft in range(2):
                f = blk * 2 + ft
                for k in range(2):
                    t_do(lambda e, f=f, ft=ft, k=k, pair=pair, hi=hi:
                         e.matmul(
                             pair[:, hi * 512 + ft * 256:
                                  hi * 512 + (ft + 1) * 256],
                             w_v1[:, k, f * 128:(f + 1) * 128],
                             a_p[:, k, :], start=(k == 0), stop=(k == 1)))
            return t_inc()

        def gelu_blk(blk, tv1):
            pair, pname, hi = slots[blk % 4]
            wait("scalar", "T", tv1)
            for ft in range(2):
                f = blk * 2 + ft
                s_do(lambda e, f=f, ft=ft, pair=pair, hi=hi, blk=blk:
                     e.activation(
                         out=a_ht[:, blk % 2, (blk % 4 // 2) * 0 +
                                  ft * 256:(ft + 1) * 256]
                         if False else
                         a_ht[:, blk % 2, ((blk // 2) % 2) * 512 +
                              ft * 256:((blk // 2) % 2) * 512 +
                              (ft + 1) * 256],
                         in_=pair[:, hi * 512 + ft * 256:
                                  hi * 512 + (ft + 1) * 256],
                         func=AF.Gelu_apprx_tanh,
                         bias=c_b1c[:, f:f + 1]))
            sg = s_mark()
            slot_free[blk % 4] = ("S", sg)
            return sg

        def u2_blk(blk, sg):
            wait("tensor", "S", sg)
            for m in range(2):
                for kf in range(2):
                    f = blk * 2 + kf
                    t_do(lambda e, m=m, f=f, blk=blk, kf=kf: e.matmul(
                        pA[:, m * 512:m * 512 + 256],
                        w_u2[:, f, m * 128:(m + 1) * 128],
                        a_ht[:, blk % 2, ((blk // 2) % 2) * 512 +
                             kf * 256:((blk // 2) % 2) * 512 +
                             (kf + 1) * 256],
                        start=(blk == 0 and kf == 0),
                        stop=(blk == 15 and kf == 1)))

        tvs = {}
        sgb = {}
        for blk in range(4):
            tvs[blk] = v1_blk(blk)
        for blk in range(16):
            sgb[blk] = gelu_blk(blk, tvs[blk])
            u2_blk(blk, sgb[blk])
            if blk + 4 < 16:
                tvs[blk + 4] = v1_blk(blk + 4)
        release("pC", "S", sgb[13])
        release("pS", "S", sgb[15])
        tq2 = t_inc()
        wait("vector", "T", tq2)
        vqb = v_inc(lambda e: e.tensor_copy(
            out=a_qb,
            in_=pA[:, :].rearrange("p (m j) -> p m j", j=512)[:, :, 0:256]))
        release("pA", "V", vqb)

        wait("tensor", "V", vqb)
        for t, pair, pname in ((0, pC, "pC"), (1, pS, "pS")):
            t_need(pname)
            for nk in range(2):
                for k in range(2):
                    t_do(lambda e, nk=nk, k=k, t=t, pair=pair: e.matmul(
                        pair[:, nk * 512:(nk + 1) * 512],
                        a_qb[:, k, t * 128:(t + 1) * 128],
                        w_v2[:, k, nk * 512:(nk + 1) * 512],
                        start=(k == 0), stop=False))
                t_do(lambda e, nk=nk, t=t, pair=pair: e.matmul(
                    pair[:, nk * 512:(nk + 1) * 512],
                    r_ones[0:1, 0:128],
                    r_b2[0:1, nk * 512:(nk + 1) * 512],
                    start=False, stop=True))
            tf = t_inc()
            wait("vector", "T", tf)
            vy = v_inc(lambda e, pair=pair, t=t: e.scalar_tensor_tensor(
                out=a_y0[:, t, :], in0=pair[:, :], scalar=1.0,
                in1=a_y0[:, t, :], op0=ALU.mult, op1=ALU.add))
            release(pname, "V", vy)
            v_do(lambda e: e.drain())
            for u_ in range(2):
                v_inc(lambda e, t=t, u_=u_, lnB=lnB: e.bn_stats(
                    out=lnB[:, t, u_, :],
                    in_=a_y0[:, t, u_ * 512:(u_ + 1) * 512]))

        def make_tail(g=g, b=b, c=c, slot=slot, x0=x0):
            def tail():
                ln_combine(s_lnB, s_ln2B)
                vout = 0
                for t in range(2):
                    vout = v_inc(lambda e, t=t, x0=x0: e.tensor_scalar(
                        out=x0[:, t, :], in0=a_y0[:, t, :],
                        scalar1=s_ln2B[:, 4 + t:5 + t],
                        scalar2=s_ln2B[:, 6 + t:7 + t],
                        op0=ALU.mult, op1=ALU.add))
                v_out_apply[g] = vout
                wait("sync", "V", vout)
                t0c = c * CHUNK
                dstap = OUT[b, t0c:t0c + CHUNK, :].rearrange(
                    "(a p) j -> p a j", p=128)
                cnt["ST"] += 16
                sstore = sem["ST"]
                plan.emit("sync", lambda e, dstap=dstap, slot=slot, s=sstore:
                          e.dma_start(out=dstap, in_=a_x0[:, slot, :, :])
                          .then_inc(s, 16))
                st_of[g] = cnt["ST"]
                if g + 2 < NG:
                    wait("sync", "ST", st_of[g])
                    ld_of[g + 2] = x0_load(g + 2)
            return tail

        pending_tail[0] = make_tail()

    if pending_tail[0] is not None:
        pending_tail[0]()
        pending_tail[0] = None
    wait("sync", "ST", cnt["ST"])

    with nc.Block() as block:
        @block.gpsimd
        def _(eng):
            for fn in plan.ops["gpsimd"]:
                fn(eng)

        @block.scalar
        def _(eng):
            for fn in plan.ops["scalar"]:
                fn(eng)

        @block.vector
        def _(eng):
            for fn in plan.ops["vector"]:
                fn(eng)

        @block.tensor
        def _(eng):
            for fn in plan.ops["tensor"]:
                fn(eng)

        @block.sync
        def _(eng):
            for fn in plan.ops["sync"]:
                fn(eng)

    import os
    if os.environ.get('SBUF_DEBUG'):
        print('sbuf bytes remaining (pre-close):', nc.sbuf_bytes_remaining)
    es.close()
    return nc


_NC_CACHE = []


def kernel(L0_attn_X, L0_attn_Wq, L0_attn_Wk, L0_attn_Wv, L0_attn_W_G,
           L0_attn_W_O, L0_ffn_U1, L0_ffn_V1, L0_ffn_b1, L0_ffn_U2,
           L0_ffn_V2, L0_ffn_b2):
    X = np.ascontiguousarray(np.asarray(L0_attn_X, dtype=np.float32))
    ws = {
        "Wq": np.ascontiguousarray(np.asarray(L0_attn_Wq, dtype=np.float32)),
        "Wk": np.ascontiguousarray(np.asarray(L0_attn_Wk, dtype=np.float32)),
        "Wv": np.ascontiguousarray(np.asarray(L0_attn_Wv, dtype=np.float32)),
        "Wg": np.ascontiguousarray(np.asarray(L0_attn_W_G, dtype=np.float32)),
        "Wo": np.ascontiguousarray(np.asarray(L0_attn_W_O, dtype=np.float32)),
        "U1": np.ascontiguousarray(np.asarray(L0_ffn_U1, dtype=np.float32)),
        "V1": np.ascontiguousarray(np.asarray(L0_ffn_V1, dtype=np.float32)),
        "B1": np.ascontiguousarray(np.asarray(L0_ffn_b1, dtype=np.float32)),
        "U2": np.ascontiguousarray(np.asarray(L0_ffn_U2, dtype=np.float32)),
        "V2": np.ascontiguousarray(np.asarray(L0_ffn_V2, dtype=np.float32)),
        "B2": np.ascontiguousarray(np.asarray(L0_ffn_b2, dtype=np.float32)),
    }
    if not _NC_CACHE:
        _NC_CACHE.append(build_program())
    nc = _NC_CACHE[0]
    XTfull = np.ascontiguousarray(X.transpose(0, 2, 1))
    in_maps = []
    for core in range(NCORES):
        m = {"X": np.ascontiguousarray(X[core * 2:core * 2 + 2]),
             "XT": XTfull[core * 2:core * 2 + 2]}
        m.update(ws)
        in_maps.append(m)
    res = run_bass_kernel_spmd(nc, in_maps, core_ids=list(range(NCORES)))
    out = np.empty((16, S_LEN, D), dtype=np.float32)
    for core in range(NCORES):
        out[core * 2:core * 2 + 2] = res.results[core]["OUT"]
    return out


# version 27
# speedup vs baseline: 1.0349x; 1.0349x over previous
import math
import numpy as np
import concourse.bass as bass
import concourse.mybir as mybir
from concourse.bass_utils import run_bass_kernel_spmd

F32 = mybir.dt.float32
BF16 = mybir.dt.bfloat16
AF = mybir.ActivationFunctionType
ALU = mybir.AluOpType

NCORES = 8
B_PER_CORE = 2
S_LEN = 4096
D = 1024
H = 8
DH = 128
CHUNK = 256
NCH = S_LEN // CHUNK
NG = B_PER_CORE * NCH
TRUNC = None
DBG = None
D_FF = 4096
RANK = 256
GN_EPS = 1e-6
LN_EPS = 1e-5
ISQD = 1.0 / math.sqrt(DH)


def _consts():
    g = 1.0 - np.exp2(-5.0 - np.arange(H, dtype=np.float64))
    lg = np.log(g)
    w = np.arange(128, dtype=np.float64)
    mask = np.zeros((128, 1024), dtype=np.float64)
    for h in range(H):
        col = np.exp(lg[h] * (-(w + 1.0))) * ISQD
        tri = (w[None, :] >= w[:, None]).astype(np.float64)
        mask[:, h * 128:(h + 1) * 128] = tri * col[:, None]
    qcol = np.zeros((128, 2048), dtype=np.float64)
    for h in range(H):
        pat = np.exp(lg[h] * ((w % 128) + 1.0))
        qcol[:, h * 256:h * 256 + 128] = pat[None, :]
        qcol[:, h * 256 + 128:(h + 1) * 256] = pat[None, :]
    kdcol = np.zeros((128, 1024), dtype=np.float64)
    for h in range(H):
        colk = np.exp(lg[h] * (127.0 - w)) * ISQD
        kdcol[:, h * 128:(h + 1) * 128] = colk[:, None]
    gid = np.zeros((128, 1024), dtype=np.float64)
    for h in range(H):
        gid[:, h * 128:(h + 1) * 128] = np.exp(lg[h] * 128.0) * np.eye(128)
    ident = np.eye(128, dtype=np.float64)
    return (mask.astype(np.float32), qcol.astype(np.float32),
            kdcol.astype(np.float32), gid.astype(np.float32),
            ident.astype(np.float32))


class _Plan:
    def __init__(self):
        self.ops = {"tensor": [], "vector": [], "scalar": [], "sync": [],
                    "gpsimd": []}

    def emit(self, eng, fn):
        self.ops[eng].append(fn)


def build_program():
    nc = bass.Bass("TRN2", target_bir_lowering=False, detect_race_conditions=False)
    X = nc.dram_tensor("X", [B_PER_CORE, S_LEN, D], F32, kind="ExternalInput").ap()
    XT = nc.dram_tensor("XT", [B_PER_CORE, D, S_LEN], F32, kind="ExternalInput").ap()
    OUT = nc.dram_tensor("OUT", [B_PER_CORE, S_LEN, D], F32,
                         kind="ExternalOutput").ap()
    Wq = nc.dram_tensor("Wq", [D, D], F32, kind="ExternalInput").ap()
    Wk = nc.dram_tensor("Wk", [D, D], F32, kind="ExternalInput").ap()
    Wv = nc.dram_tensor("Wv", [D, D], F32, kind="ExternalInput").ap()
    Wg = nc.dram_tensor("Wg", [D, D], F32, kind="ExternalInput").ap()
    Wo = nc.dram_tensor("Wo", [D, D], F32, kind="ExternalInput").ap()
    U1 = nc.dram_tensor("U1", [D, RANK], F32, kind="ExternalInput").ap()
    V1 = nc.dram_tensor("V1", [RANK, D_FF], F32, kind="ExternalInput").ap()
    B1 = nc.dram_tensor("B1", [D_FF], F32, kind="ExternalInput").ap()
    U2 = nc.dram_tensor("U2", [D_FF, RANK], F32, kind="ExternalInput").ap()
    V2 = nc.dram_tensor("V2", [RANK, D], F32, kind="ExternalInput").ap()
    B2 = nc.dram_tensor("B2", [D], F32, kind="ExternalInput").ap()

    maskC, qcolC, kdcolC, gidC, idC = _consts()
    d_mask = nc.inline_tensor(maskC, "d_mask").ap()
    d_qcol = nc.inline_tensor(qcolC, "d_qcol").ap()
    d_kdcol = nc.inline_tensor(kdcolC, "d_kdcol").ap()
    d_gid = nc.inline_tensor(gidC, "d_gid").ap()
    d_id = nc.inline_tensor(idC, "d_id").ap()

    from contextlib import ExitStack
    es = ExitStack()

    def sb(name, shape, dtype):
        return es.enter_context(nc.sbuf_tensor(name, shape, dtype))

    w_q = sb("w_q", [128, 8, 1024], BF16)
    w_k = sb("w_k", [128, 8, 1024], BF16)
    w_v = sb("w_v", [128, 8, 1024], BF16)
    w_g = sb("w_g", [128, 8, 1024], BF16)
    w_o = sb("w_o", [128, 8, 1024], BF16)
    w_u1 = sb("w_u1", [128, 8, 256], BF16)
    w_v1 = sb("w_v1", [128, 2, 4096], BF16)
    w_u2 = sb("w_u2", [128, 32, 256], BF16)
    w_v2 = sb("w_v2", [128, 2, 1024], BF16)
    r_b2 = sb("r_b2", [1, 1024], BF16)
    r_ones = sb("r_ones", [1, 128], BF16)
    c_onec = sb("c_onec", [128, 1], BF16)
    c_u1s = sb("c_u1s", [128, 2], F32)
    r_u1s = sb("r_u1s", [1, 256], BF16)
    b_rstd = sb("b_rstd", [128, 256], F32)
    c_mask = sb("c_mask", [128, 1024], BF16)
    c_qcol = sb("c_qcol", [128, 2048], BF16)
    c_kdcol = sb("c_kdcol", [128, 1024], BF16)
    c_gid = sb("c_gid", [128, 1024], F32)
    c_idf = sb("c_idf", [128, 128], F32)
    c_b1c = sb("c_b1c", [128, 32], F32)
    c_idb = sb("c_idb", [128, 128], BF16)
    a_x0 = sb("a_x0", [128, 2, 2, 1024], F32)
    a_x0t2 = sb("a_x0t2", [128, 2, 8, 256], BF16)
    a_qt = sb("a_qt", [128, 8, 256], BF16)
    a_kt = sb("a_kt", [128, 8, 256], BF16)
    a_x1t = sb("a_x1t", [128, 8, 256], BF16)
    a_v = sb("a_v", [128, 2, 1024], BF16)
    a_kd = sb("a_kd", [128, 1024], BF16)
    a_am = sb("a_am", [128, 1024], BF16)
    a_sf = sb("a_sf", [128, 1024], F32)
    a_sb = sb("a_sb", [128, 1024], BF16)
    a_y = sb("a_y", [128, 2, 1024], BF16)
    a_u = sb("a_u", [128, 2, 1024], BF16)
    a_gt = sb("a_gt", [128, 8, 256], BF16)
    a_y0 = sb("a_y0", [128, 2, 1024], F32)
    a_gated = a_u
    a_ht = a_u
    a_p = a_y[:, :, 0:256]
    a_qb = a_y[:, :, 256:512]
    s_gn = sb("s_gn", [128, 48], F32)
    s_gn2 = sb("s_gn2", [128, 40], F32)
    s_ln = sb("s_ln", [128, 24], F32)
    s_ln2 = sb("s_ln2", [128, 16], F32)
    s_lnB = sb("s_lnB", [128, 24], F32)
    s_ln2B = sb("s_ln2B", [128, 16], F32)

    pA = es.enter_context(nc.psum_tensor("pA", [128, 1024], F32))
    pC = es.enter_context(nc.psum_tensor("pC", [128, 1024], F32))
    pS = es.enter_context(nc.psum_tensor("pS", [128, 1024], F32))
    pT = es.enter_context(nc.psum_tensor("pT", [128, 2048], BF16))

    sem = {}
    for name in ("LD0", "LD1", "GT0", "GT1", "WT", "CN", "ST", "T", "V", "S"):
        sem[name] = es.enter_context(nc.semaphore(name))

    plan = _Plan()
    cnt = {"LD0": 0, "LD1": 0, "GT0": 0, "GT1": 0, "WT": 0, "CN": 0,
           "ST": 0, "T": 0, "V": 0, "S": 0}

    def inc(eng_name, sem_name, fn):
        cnt[sem_name] += 1
        s = sem[sem_name]
        plan.emit(eng_name, lambda e, fn=fn, s=s: fn(e).then_inc(s, 1))
        return cnt[sem_name]

    wait_done = {}

    def wait(eng_name, sem_name, val):
        if val <= 0:
            return
        key = (eng_name, sem_name)
        if wait_done.get(key, -1) >= val:
            return
        wait_done[key] = val
        s = sem[sem_name]
        plan.emit(eng_name, lambda e, s=s, val=val: e.wait_ge(s, val))

    t_do = lambda fn: plan.emit("tensor", fn)
    v_do = lambda fn: plan.emit("vector", fn)
    s_do = lambda fn: plan.emit("scalar", fn)
    v_inc = lambda fn: inc("vector", "V", fn)
    s_inc = lambda fn: inc("scalar", "S", fn)

    def mark(eng_name, sem_name):
        cnt[sem_name] += 1
        s = sem[sem_name]
        last = plan.ops[eng_name].pop()
        plan.emit(eng_name,
                  lambda e, last=last, s=s: last(e).then_inc(s, 1))
        return cnt[sem_name]

    t_inc = lambda: mark("tensor", "T")
    s_mark = lambda: mark("scalar", "S")

    pair_free = {"pA": None, "pC": None, "pS": None,
                 "pT0": None, "pT1": None}

    def t_need(pname):
        wv = pair_free[pname]
        if wv is not None:
            wait("tensor", wv[0], wv[1])

    def release(pname, sem_name, val):
        pair_free[pname] = (sem_name, val)

    def gload(dst, src):
        cnt["WT"] += 16
        s = sem["WT"]
        plan.emit("gpsimd", lambda e, dst=dst, src=src, s=s: e.dma_start(
            out=dst, in_=src).then_inc(s, 16))

    gload(w_q[:, :, :], Wq.rearrange("(k p) j -> p k j", p=128))
    gload(w_k[:, :, :], Wk.rearrange("(k p) j -> p k j", p=128))
    gload(w_v[:, :, :], Wv.rearrange("(k p) j -> p k j", p=128))
    gload(w_g[:, :, :], Wg.rearrange("(k p) j -> p k j", p=128))
    gload(w_o[:, :, :], Wo.rearrange("(k p) j -> p k j", p=128))
    gload(w_u1[:, :, :], U1.rearrange("(k p) j -> p k j", p=128))
    gload(w_v1[:, :, :], V1.rearrange("(k p) j -> p k j", p=128))
    gload(w_u2[:, :, :], U2.rearrange("(k p) j -> p k j", p=128))
    gload(w_v2[:, :, :], V2.rearrange("(k p) j -> p k j", p=128))
    gload(r_b2[:, :], B2.rearrange("(a j) -> a j", a=1))
    gload(c_mask[:, :], d_mask)
    gload(c_qcol[:, :], d_qcol)
    gload(c_kdcol[:, :], d_kdcol)
    WT_ALL = cnt["WT"]

    def sload(dst, src):
        cnt["CN"] += 16
        s = sem["CN"]
        plan.emit("sync", lambda e, dst=dst, src=src, s=s: e.dma_start(
            out=dst, in_=src).then_inc(s, 16))
        return cnt["CN"]

    sload(c_gid[:, :], d_gid)
    for _t in range(32):
        sload(c_b1c[:, _t:_t + 1],
              B1[_t * 128:(_t + 1) * 128].rearrange("(p a) -> p a", a=1))
    LD_CONST = sload(c_idf[:, :], d_id)
    CN_ALL = cnt["CN"]

    v_onec = v_inc(lambda e: e.memset(c_onec[:, :], 1.0))
    v_ones = v_inc(lambda e: e.memset(r_ones[:, :], 1.0))
    wait("scalar", "CN", LD_CONST)
    s_idb = s_inc(lambda e: e.activation(out=c_idb[:, :], in_=c_idf[:, :],
                                         func=AF.Copy))

    t_need("pA")
    wait("tensor", "WT", WT_ALL)
    wait("tensor", "CN", LD_CONST)
    wait("tensor", "V", v_onec)
    for m in range(2):
        for k in range(8):
            t_do(lambda e, m=m, k=k: e.matmul(
                pA[:, m:m + 1], w_u1[:, k, m * 128:(m + 1) * 128],
                c_onec[:, :], start=(k == 0), stop=(k == 7)))
    tu1 = t_inc()
    wait("vector", "T", tu1)
    vus = v_inc(lambda e: e.tensor_copy(out=c_u1s[:, :], in_=pA[:, 0:2]))
    wait("tensor", "V", vus)
    for m in range(2):
        t_do(lambda e, m=m: e.transpose(
            pA[0:1, 128 + m * 256:256 + m * 256], c_u1s[:, m:m + 1],
            c_idf[:, :]))
    tu2 = t_inc()
    wait("scalar", "T", tu2)
    for m in range(2):
        s_do(lambda e, m=m: e.activation(
            out=r_u1s[0:1, m * 128:(m + 1) * 128],
            in_=pA[0:1, 128 + m * 256:256 + m * 256], func=AF.Copy))
    su_ = s_mark()
    release("pA", "S", su_)

    def x0_load(g):
        b, c = divmod(g, NCH)
        t0 = c * CHUNK
        slot = g % 2
        sname = "LD0" if slot == 0 else "LD1"
        src = X[b, t0:t0 + CHUNK, :].rearrange("(a p) j -> p a j", p=128)
        cnt[sname] += 16
        s = sem[sname]
        plan.emit("sync", lambda e, src=src, slot=slot, s=s: e.dma_start(
            out=a_x0[:, slot, :, :], in_=src).then_inc(s, 16))
        return (sname, cnt[sname])

    proj_done = {}

    def ln_combine(SL=None, SL2=None):
        wait("vector", "V", cnt["V"])
        SL = s_ln if SL is None else SL
        SL2 = s_ln2 if SL2 is None else SL2
        L = SL[:, :]
        me0 = L[:, 1:14:12]
        mo0 = L[:, 4:17:12]
        me1 = L[:, 7:20:12]
        mo1 = L[:, 10:23:12]
        Me0 = L[:, 2:15:12]
        Mo0 = L[:, 5:18:12]
        Me1 = L[:, 8:21:12]
        Mo1 = L[:, 11:24:12]
        mu2 = SL2[:, 0:2]
        var2 = SL2[:, 2:4]
        rst2 = SL2[:, 4:6]
        ngm2 = SL2[:, 6:8]
        m1_ = SL2[:, 8:10]
        m2_ = SL2[:, 10:12]
        d1_ = SL2[:, 12:14]
        d2_ = SL2[:, 14:16]
        v_do(lambda e: e.tensor_add(m1_, me0, mo0))
        v_do(lambda e: e.tensor_add(m2_, me1, mo1))
        v_do(lambda e: e.tensor_sub(d1_, me0, mo0))
        v_do(lambda e: e.tensor_sub(d2_, me1, mo1))
        v_do(lambda e: e.tensor_add(var2, Me0, Mo0))
        v_do(lambda e: e.tensor_add(mu2, Me1, Mo1))
        v_do(lambda e: e.drain())
        v_do(lambda e: e.scalar_tensor_tensor(
            out=d1_, in0=d1_, scalar=0.125, in1=d1_, op0=ALU.mult,
            op1=ALU.mult))
        v_do(lambda e: e.scalar_tensor_tensor(
            out=d2_, in0=d2_, scalar=0.125, in1=d2_, op0=ALU.mult,
            op1=ALU.mult))
        v_do(lambda e: e.tensor_add(var2, var2, mu2))
        v_do(lambda e: e.drain())
        v_do(lambda e: e.tensor_sub(m2_, m1_, m2_))
        v_do(lambda e: e.tensor_scalar(
            out=var2, in0=var2, scalar1=1.0 / 1024.0, scalar2=LN_EPS,
            op0=ALU.mult, op1=ALU.add))
        v_do(lambda e: e.drain())
        v_do(lambda e: e.scalar_tensor_tensor(
            out=mu2, in0=m1_, scalar=2.0, in1=m2_, op0=ALU.mult,
            op1=ALU.subtract))
        v_do(lambda e: e.drain())
        v_do(lambda e: e.scalar_tensor_tensor(
            out=m2_, in0=m2_, scalar=0.0625, in1=m2_, op0=ALU.mult,
            op1=ALU.mult))
        v_do(lambda e: e.tensor_add(var2, var2, d1_))
        v_do(lambda e: e.drain())
        v_do(lambda e: e.tensor_scalar_mul(mu2, mu2, -0.25))
        v_do(lambda e: e.tensor_add(var2, var2, d2_))
        v_do(lambda e: e.drain())
        v_do(lambda e: e.tensor_add(var2, var2, m2_))
        v_do(lambda e: e.drain())
        vr = v_inc(lambda e: e.reciprocal(out=rst2, in_=var2))
        wait("scalar", "V", vr)
        sq = s_inc(lambda e: e.activation(out=rst2, in_=rst2,
                                          func=AF.Sqrt))
        wait("vector", "S", sq)
        return v_inc(lambda e: e.tensor_mul(ngm2, mu2, rst2))

    qk_done = {}
    pending_tail = [None]

    def x0t_load(g):
        b, c = divmod(g, NCH)
        t0 = c * CHUNK
        slot = g % 2
        sname = "GT0" if slot == 0 else "GT1"
        srcap = XT[b, :, t0:t0 + CHUNK].rearrange("(k p) t -> p k t", p=128)
        cnt[sname] += 16
        s = sem[sname]
        wt_t = proj_done.get(g - 2, 0)
        st_ = sem["T"]
        plan.emit("gpsimd", lambda e, srcap=srcap, slot=slot, s=s, wt_t=wt_t,
                  st_=st_: (
                      e.wait_ge(st_, wt_t) if wt_t else None,
                      e.dma_start(out=a_x0t2[:, slot, :, :], in_=srcap)
                      .then_inc(s, 16))[-1])
        return (sname, cnt[sname])

    ldt_of = {0: x0t_load(0), 1: x0t_load(1)}
    ld_of = {0: x0_load(0), 1: x0_load(1)}
    st_of = {}
    v_state_ready = 0
    sgn_prev = [0]
    v_out_apply = {}

    dbg_state = {"fired": False}

    def dbg_dump(stage, bufs, wsem, wval):
        if DBG != stage or dbg_state["fired"]:
            return False
        dbg_state["fired"] = True
        row = 0
        for ap_ in bufs:
            cnt["WT"] += 16
            s = sem["WT"]
            n = 1
            sh = ap_.shape
            tot = 1
            for d_ in sh[1:]:
                tot *= d_
            nrow = (tot + 1023) // 1024
            lastd = sh[-1] if len(sh) <= 2 else 1024
            dst = (OUT[0, row:row + 128 * nrow, :].rearrange(
                "(a p) j -> p a j", p=128) if nrow > 1
                else OUT[0, row:row + 128, 0:tot])
            wv, ws2, wt2 = cnt["V"], cnt["S"], cnt["T"]
            plan.emit("gpsimd", lambda e, dst=dst, ap_=ap_, s=s, wv=wv,
                      ws2=ws2, wt2=wt2: (
                          e.wait_ge(sem["V"], wv) if wv else None,
                          e.wait_ge(sem["S"], ws2) if ws2 else None,
                          e.wait_ge(sem["T"], wt2) if wt2 else None,
                          e.dma_start(out=dst, in_=ap_).then_inc(s, 16))[-1])
            row += 128 * nrow
        return True

    for g in range(NG):
        b, c = divmod(g, NCH)
        slot = g % 2
        x0 = a_x0[:, slot, :, :]

        if c == 0:
            wait("vector", "T", cnt["T"])
            v_inc(lambda e: e.memset(a_sf[:, :], 0.0))
            v_state_ready = v_inc(lambda e: e.memset(a_sb[:, :], 0.0))

        wait("tensor", ld_of[g][0], ld_of[g][1])
        if g == 0:
            wait("tensor", "WT", WT_ALL)
            wait("tensor", "CN", CN_ALL)
            wait("tensor", "S", s_idb)
            wait("tensor", "V", v_ones)

        def xt_of(gq):
            return a_x0t2[:, gq % 2, :, :]

        def qk_mms(gq, wsb, half, pair, pname):
            wait("tensor", ldt_of[gq][0], ldt_of[gq][1])
            t_need(pname)
            xt = xt_of(gq)
            for hh in range(half * 4, half * 4 + 4):
                for k in range(8):
                    t_do(lambda e, hh=hh, k=k, xt=xt, wsb=wsb, pair=pair:
                         e.matmul(
                             pair[:, (hh % 4) * 256:(hh % 4) * 256 + 256],
                             wsb[:, k, hh * 128:(hh + 1) * 128],
                             xt[:, k, :], start=(k == 0), stop=(k == 7)))
            return t_inc()

        def q_evac(half, pair, pname, tq):
            wait("vector", "T", tq)
            vq = v_inc(lambda e, half=half, pair=pair: e.tensor_tensor(
                out=a_qt[:, half * 4:half * 4 + 4, :],
                in0=pair[:, :].rearrange("p (h j) -> p h j", j=256),
                in1=c_qcol[:, half * 1024:(half + 1) * 1024]
                .rearrange("p (h j) -> p h j", j=256),
                op=ALU.mult))
            release(pname, "V", vq)
            return vq

        def k_evac(half, pair, pname, tk):
            wait("scalar", "T", tk)
            sk = s_inc(lambda e, half=half, pair=pair: e.activation(
                out=a_kt[:, half * 4:half * 4 + 4, :]
                .rearrange("p h j -> p (h j)"),
                in_=pair[:, :], func=AF.Copy))
            release(pname, "S", sk)
            return sk

        def emit_qk(gq):
            tq0 = qk_mms(gq, w_q, 0, pA, "pA")
            q_evac(0, pA, "pA", tq0)
            tq1 = qk_mms(gq, w_q, 1, pA, "pA")
            vq = q_evac(1, pA, "pA", tq1)
            tk0 = qk_mms(gq, w_k, 0, pA, "pA")
            k_evac(0, pA, "pA", tk0)
            tk1 = qk_mms(gq, w_k, 1, pA, "pA")
            sk = k_evac(1, pA, "pA", tk1)
            qk_done[gq] = (vq, sk)

        def tokproj_T(wsb, pair, t, gq):
            xt = xt_of(gq)
            for nk in range(2):
                for k in range(8):
                    t_do(lambda e, nk=nk, k=k, wsb=wsb, pair=pair, t=t,
                         xt=xt:
                         e.matmul(pair[:, nk * 512:(nk + 1) * 512],
                                  xt[:, k, t * 128:(t + 1) * 128],
                                  wsb[:, k, nk * 512:(nk + 1) * 512],
                                  start=(k == 0), stop=(k == 7)))
            return t_inc()

        if g == 0:
            emit_qk(0)
        vqB, skB = qk_done[g]

        def ktr(r, pTn):
            pth = pT[:, r * 1024:(r + 1) * 1024]
            t_need(pTn)
            wait("tensor", "S", skB)
            for h in range(8):
                t_do(lambda e, h=h, r=r, pth=pth: e.transpose(
                    pth[:, h * 128:(h + 1) * 128],
                    a_kt[:, h, r * 128:(r + 1) * 128], c_idb[:, :]))
            tkt = t_inc()
            wait("vector", "T", tkt)
            vkd = v_inc(lambda e, pth=pth: e.tensor_tensor(
                out=a_kd[:, :], in0=pth[:, :], in1=c_kdcol[:, :],
                op=ALU.mult))
            release(pTn, "V", vkd)
            return vkd

        def a_mms(r, pair, pname):
            t_need(pname)
            wait("tensor", "V", vqB)
            for h in range(8):
                t_do(lambda e, h=h, r=r, pair=pair: e.matmul(
                    pair[:, h * 128:(h + 1) * 128],
                    a_kt[:, h, r * 128:(r + 1) * 128],
                    a_qt[:, h, r * 128:(r + 1) * 128],
                    start=True, stop=True))
            ta = t_inc()
            wait("vector", "T", ta)
            vam = v_inc(lambda e, pair=pair: e.tensor_tensor(
                out=a_am[:, :], in0=pair[:, :], in1=c_mask[:, :],
                op=ALU.mult))
            release(pname, "V", vam)
            return vam

        def ret_mms(r, pair, pname, vam, sv_r):
            t_need(pname)
            wait("tensor", "V", vam)
            wait("tensor", "V", v_state_ready)
            wait("tensor", "S", sv_r)
            for h in range(8):
                t_do(lambda e, h=h, r=r, pair=pair: e.matmul(
                    pair[:, h * 128:(h + 1) * 128],
                    a_am[:, h * 128:(h + 1) * 128],
                    a_v[:, r, h * 128:(h + 1) * 128],
                    start=True, stop=False))
                t_do(lambda e, h=h, r=r, pair=pair: e.matmul(
                    pair[:, h * 128:(h + 1) * 128],
                    a_qt[:, h, r * 128:(r + 1) * 128],
                    a_sb[:, h * 128:(h + 1) * 128],
                    start=False, stop=True))
            return t_inc()

        def state_mms(r, vkd):
            t_need("pS")
            wait("tensor", "V", vkd)
            for h in range(8):
                t_do(lambda e, h=h: e.matmul(
                    pS[:, h * 128:(h + 1) * 128],
                    c_gid[:, h * 128:(h + 1) * 128],
                    a_sf[:, h * 128:(h + 1) * 128],
                    start=True, stop=False))
                t_do(lambda e, h=h, r=r: e.matmul(
                    pS[:, h * 128:(h + 1) * 128],
                    a_kd[:, h * 128:(h + 1) * 128],
                    a_v[:, r, h * 128:(h + 1) * 128],
                    start=False, stop=True))
            tst = t_inc()
            wait("vector", "T", tst)
            v_inc(lambda e: e.tensor_copy(out=a_sf[:, :], in_=pS[:, :]))
            vsr = v_inc(lambda e: e.tensor_copy(out=a_sb[:, :], in_=pS[:, :]))
            release("pS", "V", vsr)
            return tst, vsr

        def gn_block(r, pair, pname, tret, mid_v=None):
            wait("vector", "T", tret)
            for h in range(7):
                v_do(lambda e, h=h, pair=pair: e.bn_stats(
                    out=s_gn[:, h * 6:(h + 1) * 6],
                    in_=pair[:, h * 128:(h + 1) * 128]))
            vbn = v_inc(lambda e, pair=pair: e.bn_stats(
                out=s_gn[:, 42:48], in_=pair[:, 896:1024]))
            wait("vector", "V", vbn)
            gn = s_gn[:, :].rearrange("p (h s) -> p h s", s=6)
            mu_ = s_gn2[:, 0:8]
            var_ = s_gn2[:, 8:16]
            rst_ = s_gn2[:, 16:24]
            ngm_ = s_gn2[:, 24:32]
            tmp_ = s_gn2[:, 32:40]
            v_do(lambda e, gn=gn: e.tensor_sub(tmp_, gn[:, :, 1], gn[:, :, 4]))
            v_do(lambda e, gn=gn: e.tensor_add(var_, gn[:, :, 2], gn[:, :, 5]))
            v_do(lambda e, gn=gn: e.tensor_add(mu_, gn[:, :, 1], gn[:, :, 4]))
            v_do(lambda e: e.drain())
            v_do(lambda e: e.scalar_tensor_tensor(
                out=tmp_, in0=tmp_, scalar=0.25, in1=tmp_,
                op0=ALU.mult, op1=ALU.mult))
            v_do(lambda e: e.tensor_scalar_mul(mu_, mu_, 0.5))
            v_do(lambda e: e.tensor_scalar(
                out=var_, in0=var_, scalar1=1.0 / 128.0, scalar2=GN_EPS,
                op0=ALU.mult, op1=ALU.add))
            v_do(lambda e: e.drain())
            v_do(lambda e: e.tensor_add(var_, var_, tmp_))
            v_do(lambda e: e.drain())
            vrec = v_inc(lambda e: e.reciprocal(out=rst_, in_=var_))
            wait("scalar", "V", vrec)
            srt = s_inc(lambda e: e.activation(out=rst_, in_=rst_,
                                               func=AF.Sqrt))
            if mid_v is not None:
                mid_v()
            wait("vector", "S", srt)
            v_do(lambda e: e.scalar_tensor_tensor(
                out=ngm_, in0=mu_, scalar=-1.0, in1=rst_,
                op0=ALU.mult, op1=ALU.mult))
            v_do(lambda e: e.drain())
            for h in range(8):
                v_do(lambda e, h=h, r=r, pair=pair: e.tensor_scalar(
                    out=a_y[:, r, h * 128:(h + 1) * 128],
                    in0=pair[:, h * 128:(h + 1) * 128],
                    scalar1=s_gn2[:, 16 + h:17 + h],
                    scalar2=s_gn2[:, 24 + h:25 + h],
                    op0=ALU.mult, op1=ALU.add))
            vap = mark("vector", "V")
            release(pname, "V", vap)
            return vap

        def g_proj(t, pname, gq):
            t_need(pname)
            tgp = tokproj_T(w_g, pA, t, gq)
            wait("scalar", "T", tgp)
            sg = s_inc(lambda e, t=t: e.activation(
                out=a_u[:, t, :], in_=pA[:, :], func=AF.Tanh, scale=0.5))
            wait("vector", "S", sg)
            vu = v_inc(lambda e, t=t: e.scalar_tensor_tensor(
                out=a_u[:, t, :], in0=a_u[:, t, :], scalar=1.0,
                in1=pA[:, :], op0=ALU.add, op1=ALU.mult))
            release(pname, "V", vu)
            return vu

        def gated_tr(t, vg):
            pth = pT[:, t * 1024:(t + 1) * 1024]
            t_need("pT%d" % t)
            wait("tensor", "V", vg)
            for k in range(8):
                t_do(lambda e, k=k, t=t, pth=pth: e.transpose(
                    pth[:, k * 128:(k + 1) * 128],
                    a_u[:, t, k * 128:(k + 1) * 128], c_idb[:, :]))
            tgt = t_inc()
            wait("scalar", "T", tgt)
            s_ = s_inc(lambda e, t=t, pth=pth: e.activation(
                out=a_gt[:, :, t * 128:(t + 1) * 128],
                in_=pth[:, :].rearrange("p (k j) -> p k j", j=128),
                func=AF.Copy))
            release("pT%d" % t, "S", s_)
            return s_

        ln = s_ln[:, :].rearrange("p (t u s) -> p t u s", u=2, s=6)
        lnB = s_lnB[:, :].rearrange("p (t u s) -> p t u s", u=2, s=6)

        def attnout(t, pair, pname, sgt_t):
            t_need(pname)
            wait("tensor", "S", sgt_t)
            for nk in range(2):
                for k in range(8):
                    t_do(lambda e, nk=nk, k=k, pair=pair, t=t: e.matmul(
                        pair[:, nk * 512:(nk + 1) * 512],
                        a_gt[:, k, t * 128:(t + 1) * 128],
                        w_o[:, k, nk * 512:(nk + 1) * 512],
                        start=(k == 0), stop=(k == 7)))
            tao = t_inc()
            wait("vector", "T", tao)
            vy = v_inc(lambda e, pair=pair, t=t, x0=x0: e.scalar_tensor_tensor(
                out=a_y0[:, t, :], in0=pair[:, :], scalar=1.0,
                in1=x0[:, t, :], op0=ALU.mult, op1=ALU.add))
            release(pname, "V", vy)
            v_do(lambda e: e.drain())
            for u_ in range(2):
                v_inc(lambda e, t=t, u_=u_, ln=ln: e.bn_stats(
                    out=ln[:, t, u_, :],
                    in_=a_y0[:, t, u_ * 512:(u_ + 1) * 512]))
            return vy

        vkd0 = ktr(0, "pT0")
        t_need("pA")
        tvp0 = tokproj_T(w_v, pA, 0, g)
        wait("scalar", "T", tvp0)
        sv0 = s_inc(lambda e: e.activation(out=a_v[:, 0, :], in_=pA[:, :],
                                           func=AF.Copy))
        release("pA", "S", sv0)
        vam0 = a_mms(0, pC, "pC")
        t_need("pA")
        tvp1 = tokproj_T(w_v, pA, 1, g)
        wait("scalar", "T", tvp1)
        sv1 = s_inc(lambda e: e.activation(out=a_v[:, 1, :], in_=pA[:, :],
                                           func=AF.Copy))
        release("pA", "S", sv1)
        tret0 = ret_mms(0, pC, "pC", vam0, sv0)
        tst0, v_state_ready = state_mms(0, vkd0)
        vkd1 = ktr(1, "pT1")
        gn_block(0, pC, "pC", tret0)
        vu0 = g_proj(0, "pA", g)
        v_do(lambda e: e.drain())
        vg0 = v_inc(lambda e: e.scalar_tensor_tensor(
            out=a_u[:, 0, :], in0=a_u[:, 0, :], scalar=0.5,
            in1=a_y[:, 0, :], op0=ALU.mult, op1=ALU.mult))
        vam1 = a_mms(1, pS, "pS")
        vu1 = g_proj(1, "pA", g)
        proj_done[g] = cnt["T"]
        if g + 2 < NG:
            ldt_of[g + 2] = x0t_load(g + 2)
        tret1 = ret_mms(1, pC, "pC", vam1, sv1)
        gatedT0 = gated_tr(0, vg0)
        tst1, v_state_ready = state_mms(1, vkd1)
        nxt = g + 1 < NG
        if nxt:
            tqa = qk_mms(g + 1, w_q, 0, pA, "pA")
            mid = lambda: q_evac(0, pA, "pA", tqa)
        else:
            mid = None
        gn_block(1, pC, "pC", tret1, mid_v=mid)
        v_do(lambda e: e.drain())
        vg1 = v_inc(lambda e: e.scalar_tensor_tensor(
            out=a_u[:, 1, :], in0=a_u[:, 1, :], scalar=0.5,
            in1=a_y[:, 1, :], op0=ALU.mult, op1=ALU.mult))
        if nxt:
            tqb = qk_mms(g + 1, w_q, 1, pS, "pS")
            tka = qk_mms(g + 1, w_k, 0, pA, "pA")
            vqn = q_evac(1, pS, "pS", tqb)
            skn0 = k_evac(0, pA, "pA", tka)
        if pending_tail[0] is not None:
            pending_tail[0]()
            pending_tail[0] = None
        vy00 = attnout(0, pC, "pC", gatedT0)
        gatedT1 = gated_tr(1, vg1)
        vy01 = attnout(1, pS, "pS", gatedT1)
        if nxt:
            tkb = qk_mms(g + 1, w_k, 1, pA, "pA")
            skn = k_evac(1, pA, "pA", tkb)
            qk_done[g + 1] = (vqn, skn)

        vln1 = ln_combine()

        vx1t = []
        for t, pair, pname in ((0, pC, "pC"), (1, pS, "pS")):
            t_need(pname)
            for k in range(8):
                t_do(lambda e, k=k, t=t, pair=pair: e.transpose(
                    pair[:, k * 128:(k + 1) * 128],
                    a_y0[:, t, k * 128:(k + 1) * 128], c_idf[:, :]))
            tx1 = t_inc()
            wait("scalar", "T", tx1)
            s_ = s_inc(lambda e, pair=pair, t=t: e.activation(
                out=a_x1t[:, :, t * 128:(t + 1) * 128],
                in_=pair[:, :].rearrange("p (k j) -> p k j", j=128),
                func=AF.Copy))
            vx1t.append(s_)
            release(pname, "S", s_)

        t_need("pS")
        wait("tensor", "S", vx1t[1])
        for k in range(8):
            t_do(lambda e, k=k: e.matmul(
                pS[:, 0:256], w_u1[:, k, 0:128],
                a_x1t[:, k, :], start=(k == 0), stop=False))
        t_need("pA")
        wait("tensor", "V", vln1)
        for i_, sc in enumerate((0, 1, 4, 5)):
            t_do(lambda e, i_=i_, sc=sc: e.transpose(
                pA[0:1, 256 + (i_ % 2) * 128 + (i_ // 2) * 512:
                   384 + (i_ % 2) * 128 + (i_ // 2) * 512],
                s_ln2[:, sc:sc + 1], c_idf[:, :]))
        trr = t_inc()
        wait("vector", "T", trr)
        v_do(lambda e: e.tensor_copy(out=a_am[0:1, 0:256],
                                     in_=pA[0:1, 256:512]))
        v_do(lambda e: e.tensor_copy(out=a_am[0:1, 256:512],
                                     in_=pA[0:1, 768:1024]))
        vrr = mark("vector", "V")
        wait("tensor", "V", vrr)
        t_do(lambda e: e.matmul(
            pS[:, 0:256], r_u1s[0:1, 0:128],
            a_am[0:1, 0:256], start=False, stop=True))
        for k in range(8):
            t_do(lambda e, k=k: e.matmul(
                pS[:, 256:512], w_u1[:, k, 128:256],
                a_x1t[:, k, :], start=(k == 0), stop=False))
        t_do(lambda e: e.matmul(
            pA[:, 256:512], r_ones[0:1, 0:128],
            a_am[0:1, 256:512], start=True, stop=True))
        tb_ = t_inc()
        t_do(lambda e: e.matmul(
            pS[:, 256:512], r_u1s[0:1, 128:256],
            a_am[0:1, 0:256], start=False, stop=True))
        tp = t_inc()
        wait("scalar", "T", tb_)
        se_ = s_inc(lambda e: e.activation(out=b_rstd[:, :],
                                           in_=pA[:, 256:512], func=AF.Copy))
        wait("vector", "S", se_)
        wait("vector", "T", tp)
        v_do(lambda e: e.tensor_mul(a_p[:, 0, :], pS[:, 0:256],
                                    b_rstd[:, :]))
        vp = v_inc(lambda e: e.tensor_mul(a_p[:, 1, :], pS[:, 256:512],
                                          b_rstd[:, :]))
        release("pS", "V", vp)
        for t in range(2):
            v_inc(lambda e, t=t: e.tensor_scalar(
                out=a_y0[:, t, :], in0=a_y0[:, t, :],
                scalar1=s_ln2[:, 4 + t:5 + t],
                scalar2=s_ln2[:, 6 + t:7 + t],
                op0=ALU.mult, op1=ALU.add))
        slots = [(pC, "pC", 0), (pC, "pC", 1), (pS, "pS", 0), (pS, "pS", 1)]
        slot_free = {0: None, 1: None, 2: None, 3: None}

        def v1_blk(blk):
            pair, pname, hi = slots[blk % 4]
            w = slot_free[blk % 4]
            if w is not None:
                wait("tensor", w[0], w[1])
            elif pname == "pC":
                wv = pair_free["pC"]
                if wv is not None:
                    wait("tensor", wv[0], wv[1])
            else:
                wv = pair_free["pS"]
                if wv is not None:
                    wait("tensor", wv[0], wv[1])
            wait("tensor", "V", vp)
            for ft in range(2):
                f = blk * 2 + ft
                for k in range(2):
                    t_do(lambda e, f=f, ft=ft, k=k, pair=pair, hi=hi:
                         e.matmul(
                             pair[:, hi * 512 + ft * 256:
                                  hi * 512 + (ft + 1) * 256],
                             w_v1[:, k, f * 128:(f + 1) * 128],
                             a_p[:, k, :], start=(k == 0), stop=(k == 1)))
            return t_inc()

        def gelu_blk(blk, tv1):
            pair, pname, hi = slots[blk % 4]
            wait("scalar", "T", tv1)
            for ft in range(2):
                f = blk * 2 + ft
                s_do(lambda e, f=f, ft=ft, pair=pair, hi=hi, blk=blk:
                     e.activation(
                         out=a_ht[:, blk % 2, (blk % 4 // 2) * 0 +
                                  ft * 256:(ft + 1) * 256]
                         if False else
                         a_ht[:, blk % 2, ((blk // 2) % 2) * 512 +
                              ft * 256:((blk // 2) % 2) * 512 +
                              (ft + 1) * 256],
                         in_=pair[:, hi * 512 + ft * 256:
                                  hi * 512 + (ft + 1) * 256],
                         func=AF.Gelu_apprx_tanh,
                         bias=c_b1c[:, f:f + 1]))
            sg = s_mark()
            slot_free[blk % 4] = ("S", sg)
            return sg

        def u2_blk(blk, sg):
            wait("tensor", "S", sg)
            for m in range(2):
                for kf in range(2):
                    f = blk * 2 + kf
                    t_do(lambda e, m=m, f=f, blk=blk, kf=kf: e.matmul(
                        pA[:, m * 512:m * 512 + 256],
                        w_u2[:, f, m * 128:(m + 1) * 128],
                        a_ht[:, blk % 2, ((blk // 2) % 2) * 512 +
                             kf * 256:((blk // 2) % 2) * 512 +
                             (kf + 1) * 256],
                        start=(blk == 0 and kf == 0),
                        stop=(blk == 15 and kf == 1)))

        tvs = {}
        sgb = {}
        for blk in range(4):
            tvs[blk] = v1_blk(blk)
        for blk in range(16):
            sgb[blk] = gelu_blk(blk, tvs[blk])
            u2_blk(blk, sgb[blk])
            if blk + 4 < 16:
                tvs[blk + 4] = v1_blk(blk + 4)
        release("pC", "S", sgb[13])
        release("pS", "S", sgb[15])
        tq2 = t_inc()
        wait("vector", "T", tq2)
        vqb = v_inc(lambda e: e.tensor_copy(
            out=a_qb,
            in_=pA[:, :].rearrange("p (m j) -> p m j", j=512)[:, :, 0:256]))
        release("pA", "V", vqb)

        wait("tensor", "V", vqb)
        for t, pair, pname in ((0, pC, "pC"), (1, pS, "pS")):
            t_need(pname)
            for nk in range(2):
                for k in range(2):
                    t_do(lambda e, nk=nk, k=k, t=t, pair=pair: e.matmul(
                        pair[:, nk * 512:(nk + 1) * 512],
                        a_qb[:, k, t * 128:(t + 1) * 128],
                        w_v2[:, k, nk * 512:(nk + 1) * 512],
                        start=(k == 0), stop=False))
                t_do(lambda e, nk=nk, t=t, pair=pair: e.matmul(
                    pair[:, nk * 512:(nk + 1) * 512],
                    r_ones[0:1, 0:128],
                    r_b2[0:1, nk * 512:(nk + 1) * 512],
                    start=False, stop=True))
            tf = t_inc()
            wait("vector", "T", tf)
            vy = v_inc(lambda e, pair=pair, t=t: e.scalar_tensor_tensor(
                out=a_y0[:, t, :], in0=pair[:, :], scalar=1.0,
                in1=a_y0[:, t, :], op0=ALU.mult, op1=ALU.add))
            release(pname, "V", vy)
            v_do(lambda e: e.drain())
            for u_ in range(2):
                v_inc(lambda e, t=t, u_=u_, lnB=lnB: e.bn_stats(
                    out=lnB[:, t, u_, :],
                    in_=a_y0[:, t, u_ * 512:(u_ + 1) * 512]))

        def make_tail(g=g, b=b, c=c, slot=slot, x0=x0):
            def tail():
                ln_combine(s_lnB, s_ln2B)
                vout = 0
                for t in range(2):
                    vout = v_inc(lambda e, t=t, x0=x0: e.tensor_scalar(
                        out=x0[:, t, :], in0=a_y0[:, t, :],
                        scalar1=s_ln2B[:, 4 + t:5 + t],
                        scalar2=s_ln2B[:, 6 + t:7 + t],
                        op0=ALU.mult, op1=ALU.add))
                v_out_apply[g] = vout
                wait("sync", "V", vout)
                t0c = c * CHUNK
                dstap = OUT[b, t0c:t0c + CHUNK, :].rearrange(
                    "(a p) j -> p a j", p=128)
                cnt["ST"] += 16
                sstore = sem["ST"]
                plan.emit("sync", lambda e, dstap=dstap, slot=slot, s=sstore:
                          e.dma_start(out=dstap, in_=a_x0[:, slot, :, :])
                          .then_inc(s, 16))
                st_of[g] = cnt["ST"]
                if g + 2 < NG:
                    wait("sync", "ST", st_of[g])
                    ld_of[g + 2] = x0_load(g + 2)
            return tail

        pending_tail[0] = make_tail()

    if pending_tail[0] is not None:
        pending_tail[0]()
        pending_tail[0] = None
    wait("sync", "ST", cnt["ST"])

    with nc.Block() as block:
        @block.gpsimd
        def _(eng):
            for fn in plan.ops["gpsimd"]:
                fn(eng)

        @block.scalar
        def _(eng):
            for fn in plan.ops["scalar"]:
                fn(eng)

        @block.vector
        def _(eng):
            for fn in plan.ops["vector"]:
                fn(eng)

        @block.tensor
        def _(eng):
            for fn in plan.ops["tensor"]:
                fn(eng)

        @block.sync
        def _(eng):
            for fn in plan.ops["sync"]:
                fn(eng)

    import os
    if os.environ.get('SBUF_DEBUG'):
        print('sbuf bytes remaining (pre-close):', nc.sbuf_bytes_remaining)
    es.close()
    return nc


_NC_CACHE = []


def kernel(L0_attn_X, L0_attn_Wq, L0_attn_Wk, L0_attn_Wv, L0_attn_W_G,
           L0_attn_W_O, L0_ffn_U1, L0_ffn_V1, L0_ffn_b1, L0_ffn_U2,
           L0_ffn_V2, L0_ffn_b2):
    X = np.ascontiguousarray(np.asarray(L0_attn_X, dtype=np.float32))
    ws = {
        "Wq": np.ascontiguousarray(np.asarray(L0_attn_Wq, dtype=np.float32)),
        "Wk": np.ascontiguousarray(np.asarray(L0_attn_Wk, dtype=np.float32)),
        "Wv": np.ascontiguousarray(np.asarray(L0_attn_Wv, dtype=np.float32)),
        "Wg": np.ascontiguousarray(np.asarray(L0_attn_W_G, dtype=np.float32)),
        "Wo": np.ascontiguousarray(np.asarray(L0_attn_W_O, dtype=np.float32)),
        "U1": np.ascontiguousarray(np.asarray(L0_ffn_U1, dtype=np.float32)),
        "V1": np.ascontiguousarray(np.asarray(L0_ffn_V1, dtype=np.float32)),
        "B1": np.ascontiguousarray(np.asarray(L0_ffn_b1, dtype=np.float32)),
        "U2": np.ascontiguousarray(np.asarray(L0_ffn_U2, dtype=np.float32)),
        "V2": np.ascontiguousarray(np.asarray(L0_ffn_V2, dtype=np.float32)),
        "B2": np.ascontiguousarray(np.asarray(L0_ffn_b2, dtype=np.float32)),
    }
    if not _NC_CACHE:
        _NC_CACHE.append(build_program())
    nc = _NC_CACHE[0]
    XTfull = np.ascontiguousarray(X.transpose(0, 2, 1))
    in_maps = []
    for core in range(NCORES):
        m = {"X": np.ascontiguousarray(X[core * 2:core * 2 + 2]),
             "XT": XTfull[core * 2:core * 2 + 2]}
        m.update(ws)
        in_maps.append(m)
    res = run_bass_kernel_spmd(nc, in_maps, core_ids=list(range(NCORES)))
    out = np.empty((16, S_LEN, D), dtype=np.float32)
    for core in range(NCORES):
        out[core * 2:core * 2 + 2] = res.results[core]["OUT"]
    return out


# version 28 (final)
# speedup vs baseline: 1.0474x; 1.0121x over previous
import math
import numpy as np
import concourse.bass as bass
import concourse.mybir as mybir
from concourse.bass_utils import run_bass_kernel_spmd

F32 = mybir.dt.float32
BF16 = mybir.dt.bfloat16
AF = mybir.ActivationFunctionType
ALU = mybir.AluOpType

NCORES = 8
B_PER_CORE = 2
S_LEN = 4096
D = 1024
H = 8
DH = 128
CHUNK = 256
NCH = S_LEN // CHUNK
NG = B_PER_CORE * NCH
TRUNC = None
DBG = None
D_FF = 4096
RANK = 256
GN_EPS = 1e-6
LN_EPS = 1e-5
ISQD = 1.0 / math.sqrt(DH)


def _consts():
    g = 1.0 - np.exp2(-5.0 - np.arange(H, dtype=np.float64))
    lg = np.log(g)
    w = np.arange(128, dtype=np.float64)
    mask = np.zeros((128, 1024), dtype=np.float64)
    for h in range(H):
        col = np.exp(lg[h] * (-(w + 1.0))) * ISQD
        tri = (w[None, :] >= w[:, None]).astype(np.float64)
        mask[:, h * 128:(h + 1) * 128] = tri * col[:, None]
    qcol = np.zeros((128, 2048), dtype=np.float64)
    for h in range(H):
        pat = np.exp(lg[h] * ((w % 128) + 1.0))
        qcol[:, h * 256:h * 256 + 128] = pat[None, :]
        qcol[:, h * 256 + 128:(h + 1) * 256] = pat[None, :]
    kdcol = np.zeros((128, 1024), dtype=np.float64)
    for h in range(H):
        colk = np.exp(lg[h] * (127.0 - w)) * ISQD
        kdcol[:, h * 128:(h + 1) * 128] = colk[:, None]
    gid = np.zeros((128, 1024), dtype=np.float64)
    for h in range(H):
        gid[:, h * 128:(h + 1) * 128] = np.exp(lg[h] * 128.0) * np.eye(128)
    ident = np.eye(128, dtype=np.float64)
    return (mask.astype(np.float32), qcol.astype(np.float32),
            kdcol.astype(np.float32), gid.astype(np.float32),
            ident.astype(np.float32))


class _Plan:
    def __init__(self):
        self.ops = {"tensor": [], "vector": [], "scalar": [], "sync": [],
                    "gpsimd": []}

    def emit(self, eng, fn):
        self.ops[eng].append(fn)


def build_program():
    nc = bass.Bass("TRN2", target_bir_lowering=False, detect_race_conditions=False)
    X = nc.dram_tensor("X", [B_PER_CORE, S_LEN, D], F32, kind="ExternalInput").ap()
    XT = nc.dram_tensor("XT", [B_PER_CORE, D, S_LEN], F32, kind="ExternalInput").ap()
    OUT = nc.dram_tensor("OUT", [B_PER_CORE, S_LEN, D], F32,
                         kind="ExternalOutput").ap()
    Wq = nc.dram_tensor("Wq", [D, D], F32, kind="ExternalInput").ap()
    Wk = nc.dram_tensor("Wk", [D, D], F32, kind="ExternalInput").ap()
    Wv = nc.dram_tensor("Wv", [D, D], F32, kind="ExternalInput").ap()
    Wg = nc.dram_tensor("Wg", [D, D], F32, kind="ExternalInput").ap()
    Wo = nc.dram_tensor("Wo", [D, D], F32, kind="ExternalInput").ap()
    U1 = nc.dram_tensor("U1", [D, RANK], F32, kind="ExternalInput").ap()
    V1 = nc.dram_tensor("V1", [RANK, D_FF], F32, kind="ExternalInput").ap()
    B1 = nc.dram_tensor("B1", [D_FF], F32, kind="ExternalInput").ap()
    U2 = nc.dram_tensor("U2", [D_FF, RANK], F32, kind="ExternalInput").ap()
    V2 = nc.dram_tensor("V2", [RANK, D], F32, kind="ExternalInput").ap()
    B2 = nc.dram_tensor("B2", [D], F32, kind="ExternalInput").ap()

    maskC, qcolC, kdcolC, gidC, idC = _consts()
    d_mask = nc.inline_tensor(maskC, "d_mask").ap()
    d_qcol = nc.inline_tensor(qcolC, "d_qcol").ap()
    d_kdcol = nc.inline_tensor(kdcolC, "d_kdcol").ap()
    d_gid = nc.inline_tensor(gidC, "d_gid").ap()
    d_id = nc.inline_tensor(idC, "d_id").ap()

    from contextlib import ExitStack
    es = ExitStack()

    def sb(name, shape, dtype):
        return es.enter_context(nc.sbuf_tensor(name, shape, dtype))

    w_q = sb("w_q", [128, 8, 1024], BF16)
    w_k = sb("w_k", [128, 8, 1024], BF16)
    w_v = sb("w_v", [128, 8, 1024], BF16)
    w_g = sb("w_g", [128, 8, 1024], BF16)
    w_o = sb("w_o", [128, 8, 1024], BF16)
    w_u1 = sb("w_u1", [128, 8, 256], BF16)
    w_v1 = sb("w_v1", [128, 2, 4096], BF16)
    w_u2 = sb("w_u2", [128, 32, 256], BF16)
    w_v2 = sb("w_v2", [128, 2, 1024], BF16)
    r_b2 = sb("r_b2", [1, 1024], BF16)
    r_ones = sb("r_ones", [1, 128], BF16)
    c_onec = sb("c_onec", [128, 1], BF16)
    c_u1s = sb("c_u1s", [128, 2], F32)
    r_u1s = sb("r_u1s", [1, 256], BF16)
    b_rstd = sb("b_rstd", [128, 256], F32)
    c_mask = sb("c_mask", [128, 1024], BF16)
    c_qcol = sb("c_qcol", [128, 2048], BF16)
    c_kdcol = sb("c_kdcol", [128, 1024], BF16)
    c_gid = sb("c_gid", [128, 1024], F32)
    c_idf = sb("c_idf", [128, 128], F32)
    c_b1c = sb("c_b1c", [128, 32], F32)
    c_idb = sb("c_idb", [128, 128], BF16)
    a_x0 = sb("a_x0", [128, 2, 2, 1024], F32)
    a_x0t2 = sb("a_x0t2", [128, 2, 8, 256], BF16)
    a_qt = sb("a_qt", [128, 8, 256], BF16)
    a_kt = sb("a_kt", [128, 8, 256], BF16)
    a_x1t = sb("a_x1t", [128, 8, 256], BF16)
    a_v = sb("a_v", [128, 2, 1024], BF16)
    a_kd = sb("a_kd", [128, 1024], BF16)
    a_am = sb("a_am", [128, 1024], BF16)
    a_sf = sb("a_sf", [128, 1024], F32)
    a_sb = sb("a_sb", [128, 1024], BF16)
    a_y = sb("a_y", [128, 2, 1024], BF16)
    a_u = sb("a_u", [128, 2, 1024], BF16)
    a_gt = sb("a_gt", [128, 8, 256], BF16)
    a_y0 = sb("a_y0", [128, 2, 1024], F32)
    a_gated = a_u
    a_ht = a_u
    a_p = a_y[:, :, 0:256]
    a_qb = a_y[:, :, 256:512]
    s_gn = sb("s_gn", [128, 48], F32)
    s_gn2 = sb("s_gn2", [128, 40], F32)
    s_ln = sb("s_ln", [128, 24], F32)
    s_ln2 = sb("s_ln2", [128, 16], F32)
    s_lnB = sb("s_lnB", [128, 24], F32)
    s_ln2B = sb("s_ln2B", [128, 16], F32)

    pA = es.enter_context(nc.psum_tensor("pA", [128, 1024], F32))
    pC = es.enter_context(nc.psum_tensor("pC", [128, 1024], F32))
    pS = es.enter_context(nc.psum_tensor("pS", [128, 1024], F32))
    pT = es.enter_context(nc.psum_tensor("pT", [128, 2048], BF16))

    sem = {}
    for name in ("LD0", "LD1", "GT0", "GT1", "WT", "CN", "ST", "T", "V", "S"):
        sem[name] = es.enter_context(nc.semaphore(name))

    plan = _Plan()
    cnt = {"LD0": 0, "LD1": 0, "GT0": 0, "GT1": 0, "WT": 0, "CN": 0,
           "ST": 0, "T": 0, "V": 0, "S": 0}

    def inc(eng_name, sem_name, fn):
        cnt[sem_name] += 1
        s = sem[sem_name]
        plan.emit(eng_name, lambda e, fn=fn, s=s: fn(e).then_inc(s, 1))
        return cnt[sem_name]

    wait_done = {}

    def wait(eng_name, sem_name, val):
        if val <= 0:
            return
        key = (eng_name, sem_name)
        if wait_done.get(key, -1) >= val:
            return
        wait_done[key] = val
        s = sem[sem_name]
        plan.emit(eng_name, lambda e, s=s, val=val: e.wait_ge(s, val))

    t_do = lambda fn: plan.emit("tensor", fn)
    v_do = lambda fn: plan.emit("vector", fn)
    s_do = lambda fn: plan.emit("scalar", fn)
    v_inc = lambda fn: inc("vector", "V", fn)
    s_inc = lambda fn: inc("scalar", "S", fn)

    def mark(eng_name, sem_name):
        cnt[sem_name] += 1
        s = sem[sem_name]
        last = plan.ops[eng_name].pop()
        plan.emit(eng_name,
                  lambda e, last=last, s=s: last(e).then_inc(s, 1))
        return cnt[sem_name]

    t_inc = lambda: mark("tensor", "T")
    s_mark = lambda: mark("scalar", "S")

    pair_free = {"pA": None, "pC": None, "pS": None,
                 "pT0": None, "pT1": None}

    def t_need(pname):
        wv = pair_free[pname]
        if wv is not None:
            wait("tensor", wv[0], wv[1])

    def release(pname, sem_name, val):
        pair_free[pname] = (sem_name, val)

    def gload(dst, src):
        cnt["WT"] += 16
        s = sem["WT"]
        plan.emit("gpsimd", lambda e, dst=dst, src=src, s=s: e.dma_start(
            out=dst, in_=src).then_inc(s, 16))

    gload(w_q[:, :, :], Wq.rearrange("(k p) j -> p k j", p=128))
    gload(w_k[:, :, :], Wk.rearrange("(k p) j -> p k j", p=128))
    gload(w_v[:, :, :], Wv.rearrange("(k p) j -> p k j", p=128))
    gload(w_g[:, :, :], Wg.rearrange("(k p) j -> p k j", p=128))
    gload(w_o[:, :, :], Wo.rearrange("(k p) j -> p k j", p=128))
    gload(w_u1[:, :, :], U1.rearrange("(k p) j -> p k j", p=128))
    gload(w_v1[:, :, :], V1.rearrange("(k p) j -> p k j", p=128))
    gload(w_u2[:, :, :], U2.rearrange("(k p) j -> p k j", p=128))
    gload(w_v2[:, :, :], V2.rearrange("(k p) j -> p k j", p=128))
    gload(r_b2[:, :], B2.rearrange("(a j) -> a j", a=1))
    gload(c_mask[:, :], d_mask)
    gload(c_qcol[:, :], d_qcol)
    gload(c_kdcol[:, :], d_kdcol)
    WT_ALL = cnt["WT"]

    def sload(dst, src):
        cnt["CN"] += 16
        s = sem["CN"]
        plan.emit("sync", lambda e, dst=dst, src=src, s=s: e.dma_start(
            out=dst, in_=src).then_inc(s, 16))
        return cnt["CN"]

    sload(c_gid[:, :], d_gid)
    for _t in range(32):
        sload(c_b1c[:, _t:_t + 1],
              B1[_t * 128:(_t + 1) * 128].rearrange("(p a) -> p a", a=1))
    LD_CONST = sload(c_idf[:, :], d_id)
    CN_ALL = cnt["CN"]

    v_onec = v_inc(lambda e: e.memset(c_onec[:, :], 1.0))
    v_ones = v_inc(lambda e: e.memset(r_ones[:, :], 1.0))
    wait("scalar", "CN", LD_CONST)
    s_idb = s_inc(lambda e: e.activation(out=c_idb[:, :], in_=c_idf[:, :],
                                         func=AF.Copy))

    t_need("pA")
    wait("tensor", "WT", WT_ALL)
    wait("tensor", "CN", LD_CONST)
    wait("tensor", "V", v_onec)
    for m in range(2):
        for k in range(8):
            t_do(lambda e, m=m, k=k: e.matmul(
                pA[:, m:m + 1], w_u1[:, k, m * 128:(m + 1) * 128],
                c_onec[:, :], start=(k == 0), stop=(k == 7)))
    tu1 = t_inc()
    wait("vector", "T", tu1)
    vus = v_inc(lambda e: e.tensor_copy(out=c_u1s[:, :], in_=pA[:, 0:2]))
    wait("tensor", "V", vus)
    for m in range(2):
        t_do(lambda e, m=m: e.transpose(
            pA[0:1, 128 + m * 256:256 + m * 256], c_u1s[:, m:m + 1],
            c_idf[:, :]))
    tu2 = t_inc()
    wait("scalar", "T", tu2)
    for m in range(2):
        s_do(lambda e, m=m: e.activation(
            out=r_u1s[0:1, m * 128:(m + 1) * 128],
            in_=pA[0:1, 128 + m * 256:256 + m * 256], func=AF.Copy))
    su_ = s_mark()
    release("pA", "S", su_)

    def x0_load(g):
        b, c = divmod(g, NCH)
        t0 = c * CHUNK
        slot = g % 2
        sname = "LD0" if slot == 0 else "LD1"
        src = X[b, t0:t0 + CHUNK, :].rearrange("(a p) j -> p a j", p=128)
        cnt[sname] += 16
        s = sem[sname]
        plan.emit("sync", lambda e, src=src, slot=slot, s=s: e.dma_start(
            out=a_x0[:, slot, :, :], in_=src).then_inc(s, 16))
        return (sname, cnt[sname])

    proj_done = {}

    def ln_combine(SL=None, SL2=None):
        wait("vector", "V", cnt["V"])
        SL = s_ln if SL is None else SL
        SL2 = s_ln2 if SL2 is None else SL2
        L = SL[:, :]
        me0 = L[:, 1:14:12]
        mo0 = L[:, 4:17:12]
        me1 = L[:, 7:20:12]
        mo1 = L[:, 10:23:12]
        Me0 = L[:, 2:15:12]
        Mo0 = L[:, 5:18:12]
        Me1 = L[:, 8:21:12]
        Mo1 = L[:, 11:24:12]
        mu2 = SL2[:, 0:2]
        var2 = SL2[:, 2:4]
        rst2 = SL2[:, 4:6]
        ngm2 = SL2[:, 6:8]
        m1_ = SL2[:, 8:10]
        m2_ = SL2[:, 10:12]
        d1_ = SL2[:, 12:14]
        d2_ = SL2[:, 14:16]
        v_do(lambda e: e.tensor_add(m1_, me0, mo0))
        v_do(lambda e: e.tensor_add(m2_, me1, mo1))
        v_do(lambda e: e.tensor_sub(d1_, me0, mo0))
        v_do(lambda e: e.tensor_sub(d2_, me1, mo1))
        v_do(lambda e: e.tensor_add(var2, Me0, Mo0))
        v_do(lambda e: e.tensor_add(mu2, Me1, Mo1))
        v_do(lambda e: e.drain())
        v_do(lambda e: e.scalar_tensor_tensor(
            out=d1_, in0=d1_, scalar=0.125, in1=d1_, op0=ALU.mult,
            op1=ALU.mult))
        v_do(lambda e: e.scalar_tensor_tensor(
            out=d2_, in0=d2_, scalar=0.125, in1=d2_, op0=ALU.mult,
            op1=ALU.mult))
        v_do(lambda e: e.tensor_add(var2, var2, mu2))
        v_do(lambda e: e.drain())
        v_do(lambda e: e.tensor_sub(m2_, m1_, m2_))
        v_do(lambda e: e.tensor_scalar(
            out=var2, in0=var2, scalar1=1.0 / 1024.0, scalar2=LN_EPS,
            op0=ALU.mult, op1=ALU.add))
        v_do(lambda e: e.drain())
        v_do(lambda e: e.scalar_tensor_tensor(
            out=mu2, in0=m1_, scalar=2.0, in1=m2_, op0=ALU.mult,
            op1=ALU.subtract))
        v_do(lambda e: e.drain())
        v_do(lambda e: e.scalar_tensor_tensor(
            out=m2_, in0=m2_, scalar=0.0625, in1=m2_, op0=ALU.mult,
            op1=ALU.mult))
        v_do(lambda e: e.tensor_add(var2, var2, d1_))
        v_do(lambda e: e.drain())
        v_do(lambda e: e.tensor_scalar_mul(mu2, mu2, -0.25))
        v_do(lambda e: e.tensor_add(var2, var2, d2_))
        v_do(lambda e: e.drain())
        v_do(lambda e: e.tensor_add(var2, var2, m2_))
        v_do(lambda e: e.drain())
        vr = v_inc(lambda e: e.reciprocal(out=rst2, in_=var2))
        wait("scalar", "V", vr)
        sq = s_inc(lambda e: e.activation(out=rst2, in_=rst2,
                                          func=AF.Sqrt))
        wait("vector", "S", sq)
        return v_inc(lambda e: e.tensor_mul(ngm2, mu2, rst2))

    qk_done = {}
    pending_tail = [None]

    def x0t_load(g):
        b, c = divmod(g, NCH)
        t0 = c * CHUNK
        slot = g % 2
        sname = "GT0" if slot == 0 else "GT1"
        srcap = XT[b, :, t0:t0 + CHUNK].rearrange("(k p) t -> p k t", p=128)
        cnt[sname] += 16
        s = sem[sname]
        wt_t = proj_done.get(g - 2, 0)
        st_ = sem["T"]
        plan.emit("gpsimd", lambda e, srcap=srcap, slot=slot, s=s, wt_t=wt_t,
                  st_=st_: (
                      e.wait_ge(st_, wt_t) if wt_t else None,
                      e.dma_start(out=a_x0t2[:, slot, :, :], in_=srcap)
                      .then_inc(s, 16))[-1])
        return (sname, cnt[sname])

    ldt_of = {0: x0t_load(0), 1: x0t_load(1)}
    ld_of = {0: x0_load(0), 1: x0_load(1)}
    st_of = {}
    v_state_ready = 0
    sgn_prev = [0]
    v_out_apply = {}

    dbg_state = {"fired": False}

    def dbg_dump(stage, bufs, wsem, wval):
        if DBG != stage or dbg_state["fired"]:
            return False
        dbg_state["fired"] = True
        row = 0
        for ap_ in bufs:
            cnt["WT"] += 16
            s = sem["WT"]
            n = 1
            sh = ap_.shape
            tot = 1
            for d_ in sh[1:]:
                tot *= d_
            nrow = (tot + 1023) // 1024
            lastd = sh[-1] if len(sh) <= 2 else 1024
            dst = (OUT[0, row:row + 128 * nrow, :].rearrange(
                "(a p) j -> p a j", p=128) if nrow > 1
                else OUT[0, row:row + 128, 0:tot])
            wv, ws2, wt2 = cnt["V"], cnt["S"], cnt["T"]
            plan.emit("gpsimd", lambda e, dst=dst, ap_=ap_, s=s, wv=wv,
                      ws2=ws2, wt2=wt2: (
                          e.wait_ge(sem["V"], wv) if wv else None,
                          e.wait_ge(sem["S"], ws2) if ws2 else None,
                          e.wait_ge(sem["T"], wt2) if wt2 else None,
                          e.dma_start(out=dst, in_=ap_).then_inc(s, 16))[-1])
            row += 128 * nrow
        return True

    for g in range(NG):
        b, c = divmod(g, NCH)
        slot = g % 2
        x0 = a_x0[:, slot, :, :]

        if c == 0:
            wait("vector", "T", cnt["T"])
            v_inc(lambda e: e.memset(a_sf[:, :], 0.0))
            v_state_ready = v_inc(lambda e: e.memset(a_sb[:, :], 0.0))

        wait("tensor", ld_of[g][0], ld_of[g][1])
        if g == 0:
            wait("tensor", "WT", WT_ALL)
            wait("tensor", "CN", CN_ALL)
            wait("tensor", "S", s_idb)
            wait("tensor", "V", v_ones)

        def xt_of(gq):
            return a_x0t2[:, gq % 2, :, :]

        def qk_mms(gq, wsb, half, pair, pname):
            wait("tensor", ldt_of[gq][0], ldt_of[gq][1])
            t_need(pname)
            xt = xt_of(gq)
            for hh in range(half * 4, half * 4 + 4):
                for k in range(8):
                    t_do(lambda e, hh=hh, k=k, xt=xt, wsb=wsb, pair=pair:
                         e.matmul(
                             pair[:, (hh % 4) * 256:(hh % 4) * 256 + 256],
                             wsb[:, k, hh * 128:(hh + 1) * 128],
                             xt[:, k, :], start=(k == 0), stop=(k == 7)))
            return t_inc()

        def q_evac(half, pair, pname, tq):
            wait("vector", "T", tq)
            vq = v_inc(lambda e, half=half, pair=pair: e.tensor_tensor(
                out=a_qt[:, half * 4:half * 4 + 4, :],
                in0=pair[:, :].rearrange("p (h j) -> p h j", j=256),
                in1=c_qcol[:, half * 1024:(half + 1) * 1024]
                .rearrange("p (h j) -> p h j", j=256),
                op=ALU.mult))
            release(pname, "V", vq)
            return vq

        def k_evac(half, pair, pname, tk):
            wait("scalar", "T", tk)
            sk = s_inc(lambda e, half=half, pair=pair: e.activation(
                out=a_kt[:, half * 4:half * 4 + 4, :]
                .rearrange("p h j -> p (h j)"),
                in_=pair[:, :], func=AF.Copy))
            release(pname, "S", sk)
            return sk

        def emit_qk(gq):
            tq0 = qk_mms(gq, w_q, 0, pA, "pA")
            q_evac(0, pA, "pA", tq0)
            tq1 = qk_mms(gq, w_q, 1, pA, "pA")
            vq = q_evac(1, pA, "pA", tq1)
            tk0 = qk_mms(gq, w_k, 0, pA, "pA")
            k_evac(0, pA, "pA", tk0)
            tk1 = qk_mms(gq, w_k, 1, pA, "pA")
            sk = k_evac(1, pA, "pA", tk1)
            qk_done[gq] = (vq, sk)

        def tokproj_T(wsb, pair, t, gq):
            xt = xt_of(gq)
            for nk in range(2):
                for k in range(8):
                    t_do(lambda e, nk=nk, k=k, wsb=wsb, pair=pair, t=t,
                         xt=xt:
                         e.matmul(pair[:, nk * 512:(nk + 1) * 512],
                                  xt[:, k, t * 128:(t + 1) * 128],
                                  wsb[:, k, nk * 512:(nk + 1) * 512],
                                  start=(k == 0), stop=(k == 7)))
            return t_inc()

        if g == 0:
            emit_qk(0)
        vqB, skB = qk_done[g]

        def ktr(r, pTn):
            pth = pT[:, r * 1024:(r + 1) * 1024]
            t_need(pTn)
            wait("tensor", "S", skB)
            for h in range(8):
                t_do(lambda e, h=h, r=r, pth=pth: e.transpose(
                    pth[:, h * 128:(h + 1) * 128],
                    a_kt[:, h, r * 128:(r + 1) * 128], c_idb[:, :]))
            tkt = t_inc()
            wait("vector", "T", tkt)
            vkd = v_inc(lambda e, pth=pth: e.tensor_tensor(
                out=a_kd[:, :], in0=pth[:, :], in1=c_kdcol[:, :],
                op=ALU.mult))
            release(pTn, "V", vkd)
            return vkd

        def a_mms(r, pair, pname):
            t_need(pname)
            wait("tensor", "V", vqB)
            for h in range(8):
                t_do(lambda e, h=h, r=r, pair=pair: e.matmul(
                    pair[:, h * 128:(h + 1) * 128],
                    a_kt[:, h, r * 128:(r + 1) * 128],
                    a_qt[:, h, r * 128:(r + 1) * 128],
                    start=True, stop=True))
            ta = t_inc()
            wait("vector", "T", ta)
            vam = v_inc(lambda e, pair=pair: e.tensor_tensor(
                out=a_am[:, :], in0=pair[:, :], in1=c_mask[:, :],
                op=ALU.mult))
            release(pname, "V", vam)
            return vam

        def ret_mms(r, pair, pname, vam, sv_r):
            t_need(pname)
            wait("tensor", "V", vam)
            wait("tensor", "V", v_state_ready)
            wait("tensor", "S", sv_r)
            for h in range(8):
                t_do(lambda e, h=h, r=r, pair=pair: e.matmul(
                    pair[:, h * 128:(h + 1) * 128],
                    a_am[:, h * 128:(h + 1) * 128],
                    a_v[:, r, h * 128:(h + 1) * 128],
                    start=True, stop=False))
                t_do(lambda e, h=h, r=r, pair=pair: e.matmul(
                    pair[:, h * 128:(h + 1) * 128],
                    a_qt[:, h, r * 128:(r + 1) * 128],
                    a_sb[:, h * 128:(h + 1) * 128],
                    start=False, stop=True))
            return t_inc()

        def state_mms(r, vkd):
            t_need("pS")
            wait("tensor", "V", vkd)
            for h in range(8):
                t_do(lambda e, h=h: e.matmul(
                    pS[:, h * 128:(h + 1) * 128],
                    c_gid[:, h * 128:(h + 1) * 128],
                    a_sf[:, h * 128:(h + 1) * 128],
                    start=True, stop=False))
                t_do(lambda e, h=h, r=r: e.matmul(
                    pS[:, h * 128:(h + 1) * 128],
                    a_kd[:, h * 128:(h + 1) * 128],
                    a_v[:, r, h * 128:(h + 1) * 128],
                    start=False, stop=True))
            tst = t_inc()
            wait("vector", "T", tst)
            v_inc(lambda e: e.tensor_copy(out=a_sf[:, :], in_=pS[:, :]))
            vsr = v_inc(lambda e: e.tensor_copy(out=a_sb[:, :], in_=pS[:, :]))
            release("pS", "V", vsr)
            return tst, vsr

        def gn_block(r, pair, pname, tret, mid_v=None):
            wait("vector", "T", tret)
            for h in range(7):
                v_do(lambda e, h=h, pair=pair: e.bn_stats(
                    out=s_gn[:, h * 6:(h + 1) * 6],
                    in_=pair[:, h * 128:(h + 1) * 128]))
            vbn = v_inc(lambda e, pair=pair: e.bn_stats(
                out=s_gn[:, 42:48], in_=pair[:, 896:1024]))
            wait("vector", "V", vbn)
            gn = s_gn[:, :].rearrange("p (h s) -> p h s", s=6)
            mu_ = s_gn2[:, 0:8]
            var_ = s_gn2[:, 8:16]
            rst_ = s_gn2[:, 16:24]
            ngm_ = s_gn2[:, 24:32]
            tmp_ = s_gn2[:, 32:40]
            v_do(lambda e, gn=gn: e.tensor_sub(tmp_, gn[:, :, 1], gn[:, :, 4]))
            v_do(lambda e, gn=gn: e.tensor_add(var_, gn[:, :, 2], gn[:, :, 5]))
            v_do(lambda e, gn=gn: e.tensor_add(mu_, gn[:, :, 1], gn[:, :, 4]))
            v_do(lambda e: e.drain())
            v_do(lambda e: e.scalar_tensor_tensor(
                out=tmp_, in0=tmp_, scalar=0.25, in1=tmp_,
                op0=ALU.mult, op1=ALU.mult))
            v_do(lambda e: e.tensor_scalar_mul(mu_, mu_, 0.5))
            v_do(lambda e: e.tensor_scalar(
                out=var_, in0=var_, scalar1=1.0 / 128.0, scalar2=GN_EPS,
                op0=ALU.mult, op1=ALU.add))
            v_do(lambda e: e.drain())
            v_do(lambda e: e.tensor_add(var_, var_, tmp_))
            v_do(lambda e: e.drain())
            vrec = v_inc(lambda e: e.reciprocal(out=rst_, in_=var_))
            wait("scalar", "V", vrec)
            srt = s_inc(lambda e: e.activation(out=rst_, in_=rst_,
                                               func=AF.Sqrt))
            if mid_v is not None:
                mid_v()
            wait("vector", "S", srt)
            v_do(lambda e: e.scalar_tensor_tensor(
                out=ngm_, in0=mu_, scalar=-1.0, in1=rst_,
                op0=ALU.mult, op1=ALU.mult))
            v_do(lambda e: e.drain())
            for h in range(8):
                v_do(lambda e, h=h, r=r, pair=pair: e.tensor_scalar(
                    out=a_y[:, r, h * 128:(h + 1) * 128],
                    in0=pair[:, h * 128:(h + 1) * 128],
                    scalar1=s_gn2[:, 16 + h:17 + h],
                    scalar2=s_gn2[:, 24 + h:25 + h],
                    op0=ALU.mult, op1=ALU.add))
            vap = mark("vector", "V")
            release(pname, "V", vap)
            return vap

        def g_proj(t, pname, gq):
            t_need(pname)
            tgp = tokproj_T(w_g, pA, t, gq)
            wait("scalar", "T", tgp)
            sg = s_inc(lambda e, t=t: e.activation(
                out=a_u[:, t, :], in_=pA[:, :], func=AF.Tanh, scale=0.5))
            wait("vector", "S", sg)
            vu = v_inc(lambda e, t=t: e.scalar_tensor_tensor(
                out=a_u[:, t, :], in0=a_u[:, t, :], scalar=1.0,
                in1=pA[:, :], op0=ALU.add, op1=ALU.mult))
            release(pname, "V", vu)
            return vu

        def gated_tr(t, vg):
            pth = pT[:, t * 1024:(t + 1) * 1024]
            t_need("pT%d" % t)
            wait("tensor", "V", vg)
            for k in range(8):
                t_do(lambda e, k=k, t=t, pth=pth: e.transpose(
                    pth[:, k * 128:(k + 1) * 128],
                    a_u[:, t, k * 128:(k + 1) * 128], c_idb[:, :]))
            tgt = t_inc()
            wait("scalar", "T", tgt)
            s_ = s_inc(lambda e, t=t, pth=pth: e.activation(
                out=a_gt[:, :, t * 128:(t + 1) * 128],
                in_=pth[:, :].rearrange("p (k j) -> p k j", j=128),
                func=AF.Copy))
            release("pT%d" % t, "S", s_)
            return s_

        ln = s_ln[:, :].rearrange("p (t u s) -> p t u s", u=2, s=6)
        lnB = s_lnB[:, :].rearrange("p (t u s) -> p t u s", u=2, s=6)

        def attnout(t, pair, pname, sgt_t):
            t_need(pname)
            wait("tensor", "S", sgt_t)
            for nk in range(2):
                for k in range(8):
                    t_do(lambda e, nk=nk, k=k, pair=pair, t=t: e.matmul(
                        pair[:, nk * 512:(nk + 1) * 512],
                        a_gt[:, k, t * 128:(t + 1) * 128],
                        w_o[:, k, nk * 512:(nk + 1) * 512],
                        start=(k == 0), stop=(k == 7)))
            tao = t_inc()
            wait("vector", "T", tao)
            vy = v_inc(lambda e, pair=pair, t=t, x0=x0: e.scalar_tensor_tensor(
                out=a_y0[:, t, :], in0=pair[:, :], scalar=1.0,
                in1=x0[:, t, :], op0=ALU.mult, op1=ALU.add))
            release(pname, "V", vy)
            v_do(lambda e: e.drain())
            for u_ in range(2):
                v_inc(lambda e, t=t, u_=u_, ln=ln: e.bn_stats(
                    out=ln[:, t, u_, :],
                    in_=a_y0[:, t, u_ * 512:(u_ + 1) * 512]))
            return vy

        vkd0 = ktr(0, "pT0")
        t_need("pA")
        tvp0 = tokproj_T(w_v, pA, 0, g)
        wait("scalar", "T", tvp0)
        sv0 = s_inc(lambda e: e.activation(out=a_v[:, 0, :], in_=pA[:, :],
                                           func=AF.Copy))
        release("pA", "S", sv0)
        vam0 = a_mms(0, pC, "pC")
        t_need("pA")
        tvp1 = tokproj_T(w_v, pA, 1, g)
        wait("scalar", "T", tvp1)
        sv1 = s_inc(lambda e: e.activation(out=a_v[:, 1, :], in_=pA[:, :],
                                           func=AF.Copy))
        release("pA", "S", sv1)
        tret0 = ret_mms(0, pC, "pC", vam0, sv0)
        tst0, v_state_ready = state_mms(0, vkd0)
        vkd1 = ktr(1, "pT1")
        gn_block(0, pC, "pC", tret0)
        vu0 = g_proj(0, "pA", g)
        v_do(lambda e: e.drain())
        vg0 = v_inc(lambda e: e.scalar_tensor_tensor(
            out=a_u[:, 0, :], in0=a_u[:, 0, :], scalar=0.5,
            in1=a_y[:, 0, :], op0=ALU.mult, op1=ALU.mult))
        vam1 = a_mms(1, pS, "pS")
        vu1 = g_proj(1, "pA", g)
        proj_done[g] = cnt["T"]
        if g + 2 < NG:
            ldt_of[g + 2] = x0t_load(g + 2)
        tret1 = ret_mms(1, pC, "pC", vam1, sv1)
        tst1, v_state_ready = state_mms(1, vkd1)
        gatedT0 = gated_tr(0, vg0)
        nxt = g + 1 < NG
        if nxt:
            tqa = qk_mms(g + 1, w_q, 0, pA, "pA")
            mid = lambda: q_evac(0, pA, "pA", tqa)
        else:
            mid = None
        gn_block(1, pC, "pC", tret1, mid_v=mid)
        v_do(lambda e: e.drain())
        vg1 = v_inc(lambda e: e.scalar_tensor_tensor(
            out=a_u[:, 1, :], in0=a_u[:, 1, :], scalar=0.5,
            in1=a_y[:, 1, :], op0=ALU.mult, op1=ALU.mult))
        if nxt:
            tqb = qk_mms(g + 1, w_q, 1, pS, "pS")
            tka = qk_mms(g + 1, w_k, 0, pA, "pA")
            vqn = q_evac(1, pS, "pS", tqb)
            skn0 = k_evac(0, pA, "pA", tka)
        if pending_tail[0] is not None:
            pending_tail[0]()
            pending_tail[0] = None
        vy00 = attnout(0, pC, "pC", gatedT0)
        gatedT1 = gated_tr(1, vg1)
        vy01 = attnout(1, pS, "pS", gatedT1)
        if nxt:
            tkb = qk_mms(g + 1, w_k, 1, pA, "pA")
            skn = k_evac(1, pA, "pA", tkb)
            qk_done[g + 1] = (vqn, skn)

        vln1 = ln_combine()

        vx1t = []
        for t, pair, pname in ((0, pC, "pC"), (1, pS, "pS")):
            t_need(pname)
            for k in range(8):
                t_do(lambda e, k=k, t=t, pair=pair: e.transpose(
                    pair[:, k * 128:(k + 1) * 128],
                    a_y0[:, t, k * 128:(k + 1) * 128], c_idf[:, :]))
            tx1 = t_inc()
            wait("scalar", "T", tx1)
            s_ = s_inc(lambda e, pair=pair, t=t: e.activation(
                out=a_x1t[:, :, t * 128:(t + 1) * 128],
                in_=pair[:, :].rearrange("p (k j) -> p k j", j=128),
                func=AF.Copy))
            vx1t.append(s_)
            release(pname, "S", s_)

        t_need("pS")
        wait("tensor", "S", vx1t[1])
        for k in range(8):
            t_do(lambda e, k=k: e.matmul(
                pS[:, 0:256], w_u1[:, k, 0:128],
                a_x1t[:, k, :], start=(k == 0), stop=False))
        t_need("pA")
        wait("tensor", "V", vln1)
        for i_, sc in enumerate((0, 1, 4, 5)):
            t_do(lambda e, i_=i_, sc=sc: e.transpose(
                pA[0:1, 256 + (i_ % 2) * 128 + (i_ // 2) * 512:
                   384 + (i_ % 2) * 128 + (i_ // 2) * 512],
                s_ln2[:, sc:sc + 1], c_idf[:, :]))
        trr = t_inc()
        wait("vector", "T", trr)
        v_do(lambda e: e.tensor_copy(out=a_am[0:1, 0:256],
                                     in_=pA[0:1, 256:512]))
        v_do(lambda e: e.tensor_copy(out=a_am[0:1, 256:512],
                                     in_=pA[0:1, 768:1024]))
        vrr = mark("vector", "V")
        wait("tensor", "V", vrr)
        t_do(lambda e: e.matmul(
            pS[:, 0:256], r_u1s[0:1, 0:128],
            a_am[0:1, 0:256], start=False, stop=True))
        for k in range(8):
            t_do(lambda e, k=k: e.matmul(
                pS[:, 256:512], w_u1[:, k, 128:256],
                a_x1t[:, k, :], start=(k == 0), stop=False))
        t_do(lambda e: e.matmul(
            pA[:, 256:512], r_ones[0:1, 0:128],
            a_am[0:1, 256:512], start=True, stop=True))
        tb_ = t_inc()
        t_do(lambda e: e.matmul(
            pS[:, 256:512], r_u1s[0:1, 128:256],
            a_am[0:1, 0:256], start=False, stop=True))
        tp = t_inc()
        wait("scalar", "T", tb_)
        se_ = s_inc(lambda e: e.activation(out=b_rstd[:, :],
                                           in_=pA[:, 256:512], func=AF.Copy))
        wait("vector", "S", se_)
        wait("vector", "T", tp)
        v_do(lambda e: e.tensor_mul(a_p[:, 0, :], pS[:, 0:256],
                                    b_rstd[:, :]))
        vp = v_inc(lambda e: e.tensor_mul(a_p[:, 1, :], pS[:, 256:512],
                                          b_rstd[:, :]))
        release("pS", "V", vp)
        for t in range(2):
            v_inc(lambda e, t=t: e.tensor_scalar(
                out=a_y0[:, t, :], in0=a_y0[:, t, :],
                scalar1=s_ln2[:, 4 + t:5 + t],
                scalar2=s_ln2[:, 6 + t:7 + t],
                op0=ALU.mult, op1=ALU.add))
        slots = [(pC, "pC", 0), (pC, "pC", 1), (pS, "pS", 0), (pS, "pS", 1)]
        slot_free = {0: None, 1: None, 2: None, 3: None}

        def v1_blk(blk):
            pair, pname, hi = slots[blk % 4]
            w = slot_free[blk % 4]
            if w is not None:
                wait("tensor", w[0], w[1])
            elif pname == "pC":
                wv = pair_free["pC"]
                if wv is not None:
                    wait("tensor", wv[0], wv[1])
            else:
                wv = pair_free["pS"]
                if wv is not None:
                    wait("tensor", wv[0], wv[1])
            wait("tensor", "V", vp)
            for ft in range(2):
                f = blk * 2 + ft
                for k in range(2):
                    t_do(lambda e, f=f, ft=ft, k=k, pair=pair, hi=hi:
                         e.matmul(
                             pair[:, hi * 512 + ft * 256:
                                  hi * 512 + (ft + 1) * 256],
                             w_v1[:, k, f * 128:(f + 1) * 128],
                             a_p[:, k, :], start=(k == 0), stop=(k == 1)))
            return t_inc()

        def gelu_blk(blk, tv1):
            pair, pname, hi = slots[blk % 4]
            wait("scalar", "T", tv1)
            for ft in range(2):
                f = blk * 2 + ft
                s_do(lambda e, f=f, ft=ft, pair=pair, hi=hi, blk=blk:
                     e.activation(
                         out=a_ht[:, blk % 2, (blk % 4 // 2) * 0 +
                                  ft * 256:(ft + 1) * 256]
                         if False else
                         a_ht[:, blk % 2, ((blk // 2) % 2) * 512 +
                              ft * 256:((blk // 2) % 2) * 512 +
                              (ft + 1) * 256],
                         in_=pair[:, hi * 512 + ft * 256:
                                  hi * 512 + (ft + 1) * 256],
                         func=AF.Gelu_apprx_tanh,
                         bias=c_b1c[:, f:f + 1]))
            sg = s_mark()
            slot_free[blk % 4] = ("S", sg)
            return sg

        def u2_blk(blk, sg):
            wait("tensor", "S", sg)
            for m in range(2):
                for kf in range(2):
                    f = blk * 2 + kf
                    t_do(lambda e, m=m, f=f, blk=blk, kf=kf: e.matmul(
                        pA[:, m * 512:m * 512 + 256],
                        w_u2[:, f, m * 128:(m + 1) * 128],
                        a_ht[:, blk % 2, ((blk // 2) % 2) * 512 +
                             kf * 256:((blk // 2) % 2) * 512 +
                             (kf + 1) * 256],
                        start=(blk == 0 and kf == 0),
                        stop=(blk == 15 and kf == 1)))

        tvs = {}
        sgb = {}
        for blk in range(4):
            tvs[blk] = v1_blk(blk)
        for blk in range(16):
            sgb[blk] = gelu_blk(blk, tvs[blk])
            u2_blk(blk, sgb[blk])
            if blk + 4 < 16:
                tvs[blk + 4] = v1_blk(blk + 4)
        release("pC", "S", sgb[13])
        release("pS", "S", sgb[15])
        tq2 = t_inc()
        wait("vector", "T", tq2)
        vqb = v_inc(lambda e: e.tensor_copy(
            out=a_qb,
            in_=pA[:, :].rearrange("p (m j) -> p m j", j=512)[:, :, 0:256]))
        release("pA", "V", vqb)

        wait("tensor", "V", vqb)
        for t, pair, pname in ((0, pC, "pC"), (1, pS, "pS")):
            t_need(pname)
            for nk in range(2):
                for k in range(2):
                    t_do(lambda e, nk=nk, k=k, t=t, pair=pair: e.matmul(
                        pair[:, nk * 512:(nk + 1) * 512],
                        a_qb[:, k, t * 128:(t + 1) * 128],
                        w_v2[:, k, nk * 512:(nk + 1) * 512],
                        start=(k == 0), stop=False))
                t_do(lambda e, nk=nk, t=t, pair=pair: e.matmul(
                    pair[:, nk * 512:(nk + 1) * 512],
                    r_ones[0:1, 0:128],
                    r_b2[0:1, nk * 512:(nk + 1) * 512],
                    start=False, stop=True))
            tf = t_inc()
            wait("vector", "T", tf)
            vy = v_inc(lambda e, pair=pair, t=t: e.scalar_tensor_tensor(
                out=a_y0[:, t, :], in0=pair[:, :], scalar=1.0,
                in1=a_y0[:, t, :], op0=ALU.mult, op1=ALU.add))
            release(pname, "V", vy)
            v_do(lambda e: e.drain())
            for u_ in range(2):
                v_inc(lambda e, t=t, u_=u_, lnB=lnB: e.bn_stats(
                    out=lnB[:, t, u_, :],
                    in_=a_y0[:, t, u_ * 512:(u_ + 1) * 512]))

        def make_tail(g=g, b=b, c=c, slot=slot, x0=x0):
            def tail():
                ln_combine(s_lnB, s_ln2B)
                vout = 0
                for t in range(2):
                    vout = v_inc(lambda e, t=t, x0=x0: e.tensor_scalar(
                        out=x0[:, t, :], in0=a_y0[:, t, :],
                        scalar1=s_ln2B[:, 4 + t:5 + t],
                        scalar2=s_ln2B[:, 6 + t:7 + t],
                        op0=ALU.mult, op1=ALU.add))
                v_out_apply[g] = vout
                wait("sync", "V", vout)
                t0c = c * CHUNK
                dstap = OUT[b, t0c:t0c + CHUNK, :].rearrange(
                    "(a p) j -> p a j", p=128)
                cnt["ST"] += 16
                sstore = sem["ST"]
                plan.emit("sync", lambda e, dstap=dstap, slot=slot, s=sstore:
                          e.dma_start(out=dstap, in_=a_x0[:, slot, :, :])
                          .then_inc(s, 16))
                st_of[g] = cnt["ST"]
                if g + 2 < NG:
                    wait("sync", "ST", st_of[g])
                    ld_of[g + 2] = x0_load(g + 2)
            return tail

        pending_tail[0] = make_tail()

    if pending_tail[0] is not None:
        pending_tail[0]()
        pending_tail[0] = None
    wait("sync", "ST", cnt["ST"])

    with nc.Block() as block:
        @block.gpsimd
        def _(eng):
            for fn in plan.ops["gpsimd"]:
                fn(eng)

        @block.scalar
        def _(eng):
            for fn in plan.ops["scalar"]:
                fn(eng)

        @block.vector
        def _(eng):
            for fn in plan.ops["vector"]:
                fn(eng)

        @block.tensor
        def _(eng):
            for fn in plan.ops["tensor"]:
                fn(eng)

        @block.sync
        def _(eng):
            for fn in plan.ops["sync"]:
                fn(eng)

    import os
    if os.environ.get('SBUF_DEBUG'):
        print('sbuf bytes remaining (pre-close):', nc.sbuf_bytes_remaining)
    es.close()
    return nc


_NC_CACHE = []


def kernel(L0_attn_X, L0_attn_Wq, L0_attn_Wk, L0_attn_Wv, L0_attn_W_G,
           L0_attn_W_O, L0_ffn_U1, L0_ffn_V1, L0_ffn_b1, L0_ffn_U2,
           L0_ffn_V2, L0_ffn_b2):
    X = np.ascontiguousarray(np.asarray(L0_attn_X, dtype=np.float32))
    ws = {
        "Wq": np.ascontiguousarray(np.asarray(L0_attn_Wq, dtype=np.float32)),
        "Wk": np.ascontiguousarray(np.asarray(L0_attn_Wk, dtype=np.float32)),
        "Wv": np.ascontiguousarray(np.asarray(L0_attn_Wv, dtype=np.float32)),
        "Wg": np.ascontiguousarray(np.asarray(L0_attn_W_G, dtype=np.float32)),
        "Wo": np.ascontiguousarray(np.asarray(L0_attn_W_O, dtype=np.float32)),
        "U1": np.ascontiguousarray(np.asarray(L0_ffn_U1, dtype=np.float32)),
        "V1": np.ascontiguousarray(np.asarray(L0_ffn_V1, dtype=np.float32)),
        "B1": np.ascontiguousarray(np.asarray(L0_ffn_b1, dtype=np.float32)),
        "U2": np.ascontiguousarray(np.asarray(L0_ffn_U2, dtype=np.float32)),
        "V2": np.ascontiguousarray(np.asarray(L0_ffn_V2, dtype=np.float32)),
        "B2": np.ascontiguousarray(np.asarray(L0_ffn_b2, dtype=np.float32)),
    }
    if not _NC_CACHE:
        _NC_CACHE.append(build_program())
    nc = _NC_CACHE[0]
    XTfull = np.ascontiguousarray(X.transpose(0, 2, 1))
    in_maps = []
    for core in range(NCORES):
        m = {"X": np.ascontiguousarray(X[core * 2:core * 2 + 2]),
             "XT": XTfull[core * 2:core * 2 + 2]}
        m.update(ws)
        in_maps.append(m)
    res = run_bass_kernel_spmd(nc, in_maps, core_ids=list(range(NCORES)))
    out = np.empty((16, S_LEN, D), dtype=np.float32)
    for core in range(NCORES):
        out[core * 2:core * 2 + 2] = res.results[core]["OUT"]
    return out
